# version 02
# baseline (speedup 1.0000x reference)
import numpy as np
from contextlib import ExitStack
import concourse.bass as bass
import concourse.mybir as mybir
from concourse.bass_utils import run_bass_kernel_spmd

F32 = mybir.dt.float32
I32 = mybir.dt.int32
AF = mybir.ActivationFunctionType
ALU = mybir.AluOpType
AX = mybir.AxisListType

B, T, D = 2, 4096, 2048
N_CORES = 8
NTOK = 1024
DT_ = 16
NCH = 512
PROJ = 5248
LN_EPS = 1e-5
L0_EPS = 1e-5
MOD_EPS = 1e-6


def fm(a2d):
    return np.ascontiguousarray(np.asarray(a2d).T.astype(np.float32, copy=False))


def colv(vec, nt=None):
    v = np.asarray(vec).astype(np.float32, copy=False)
    n = v.shape[0] // 128 if nt is None else nt
    return np.ascontiguousarray(v.reshape(n, 128).T)


class SP:

    def __init__(self, nc):
        self.nc = nc
        self.ops = []

    def ld(self, out, in_):
        self.ops.append(("sync", (lambda o=out, i=in_: self.nc.sync.dma_start(out=o, in_=i)), 16))

    def st(self, out, in_):
        self.ops.append(("gpsimd", (lambda o=out, i=in_: self.nc.gpsimd.dma_start(out=o, in_=i)), 16))

    def v(self, fn):
        self.ops.append(("vector", fn, 1))

    def s(self, fn):
        self.ops.append(("scalar", fn, 1))

    def t(self, fn):
        self.ops.append(("tensor", fn, 1))

    def g(self, fn):
        self.ops.append(("gpsimd", fn, 1))

    def build(self, sem):
        pre = []
        tot = 0
        for (_, _, inc) in self.ops:
            pre.append(tot)
            tot += inc
        nc = self.nc
        ops = self.ops
        with nc.Block() as block:
            def mk(name):
                def body(eng):
                    for i, (e, f, inc) in enumerate(ops):
                        if e != name:
                            continue
                        if pre[i] > 0:
                            eng.wait_ge(sem, pre[i])
                        f().then_inc(sem, inc)
                    if name == "gpsimd":
                        eng.wait_ge(sem, tot)
                return body
            block.sync(mk("sync"))
            block.vector(mk("vector"))
            block.scalar(mk("scalar"))
            block.tensor(mk("tensor"))
            block.gpsimd(mk("gpsimd"))



def ln_stats(nc, sp, src_sb, ntok, nt, psR, s1, s2, sq, ones128, base=0):
    nch = ntok // NCH
    for ch in range(nch):
        for k in range(nt):
            c0 = base + k * ntok + ch * NCH
            sp.t(lambda a=psR, b=ones128, c=src_sb, c0=c0, st=(k == 0), sp_=(k == nt - 1):
                 nc.tensor.matmul(a[:, 0:NCH], b[:, 0:1], c[:, c0:c0 + NCH], start=st, stop=sp_))
        sp.v(lambda a=s1, b=psR, ch=ch: nc.vector.tensor_copy(a[:, ch * NCH:(ch + 1) * NCH], b[:, 0:NCH]))
        for k in range(nt):
            c0 = base + k * ntok + ch * NCH
            sp.s(lambda a=sq, b=src_sb, c0=c0: nc.scalar.activation(out=a[:, 0:NCH], in_=b[:, c0:c0 + NCH], func=AF.Square))
            sp.t(lambda a=psR, b=ones128, c=sq, st=(k == 0), sp_=(k == nt - 1):
                 nc.tensor.matmul(a[:, 0:NCH], b[:, 0:1], c[:, 0:NCH], start=st, stop=sp_))
        sp.v(lambda a=s2, b=psR, ch=ch: nc.vector.tensor_copy(a[:, ch * NCH:(ch + 1) * NCH], b[:, 0:NCH]))


def rstd_from_s2(nc, sp, s2, rstd, dinv, eps_ap, ntok):
    sp.v(lambda: nc.vector.tensor_scalar_mul(s2[:, 0:ntok], s2[:, 0:ntok], dinv))
    sp.s(lambda: nc.scalar.activation(out=rstd[:, 0:ntok], in_=s2[:, 0:ntok], func=AF.Sqrt, bias=eps_ap))
    sp.v(lambda: nc.vector.reciprocal(rstd[:, 0:ntok], rstd[:, 0:ntok]))


def bcast_row(nc, sp, ps, ones1, vec, c0, n):
    sp.t(lambda: nc.tensor.matmul(ps[:, 0:n], ones1[:, 0:128], vec[:, c0:c0 + n], start=True, stop=True))



def build_A():
    nc = bass.Bass("TRN2", target_bir_lowering=False, debug=False)
    Xin = nc.dram_tensor("x", [D, NTOK], F32, kind="ExternalInput").ap()
    Win = nc.dram_tensor("w_in", [D, PROJ], F32, kind="ExternalInput").ap()
    Wdt = nc.dram_tensor("w_dt", [128, D], F32, kind="ExternalInput").ap()
    Bdt = nc.dram_tensor("b_dt", [128, DT_], F32, kind="ExternalInput").ap()
    G1 = nc.dram_tensor("g1", [128, DT_], F32, kind="ExternalInput").ap()
    WR = nc.dram_tensor("wr", [128, DT_], F32, kind="ExternalInput").ap()
    Oxn = nc.dram_tensor("xn0", [D, NTOK], F32, kind="ExternalOutput").ap()
    Oz = nc.dram_tensor("z", [D, NTOK], F32, kind="ExternalOutput").ap()
    Oxs = nc.dram_tensor("xs", [512, NTOK], F32, kind="ExternalOutput").ap()
    Obc = nc.dram_tensor("bc", [512, NTOK], F32, kind="ExternalOutput").ap()
    Occ = nc.dram_tensor("cc", [D, NTOK], F32, kind="ExternalOutput").ap()
    Odt = nc.dram_tensor("dt", [D, NTOK], F32, kind="ExternalOutput").ap()
    Osc = nc.dram_tensor("sc", [1, NTOK], F32, kind="ExternalOutput").ap()
    Ogb = nc.dram_tensor("gb", [D, NTOK], F32, kind="ExternalOutput").ap()

    sp = SP(nc)
    with ExitStack() as es:
        E = es.enter_context
        xsb = E(nc.sbuf_tensor("xsb", [128, DT_ * NTOK], F32))
        xnsb = E(nc.sbuf_tensor("xnsb", [128, DT_ * NTOK], F32))
        r1sb = xsb
        drsb = E(nc.sbuf_tensor("drsb", [128, NTOK], F32))
        dtwsb = E(nc.sbuf_tensor("dtwsb", [128, D], F32))
        wtile = E(nc.sbuf_tensor("wtile", [128, DT_ * 128], F32))
        sq = E(nc.sbuf_tensor("sq", [128, NCH], F32))
        outsb = E(nc.sbuf_tensor("outsb", [128, NCH], F32))
        t1 = E(nc.sbuf_tensor("t1", [128, NCH], F32))
        t2 = E(nc.sbuf_tensor("t2", [128, NCH], F32))
        vs = E(nc.sbuf_tensor("vs", [1, NTOK * 6], F32))
        ones128 = E(nc.sbuf_tensor("ones128", [128, 1], F32))
        csb = E(nc.sbuf_tensor("csb", [128, 4], F32))
        ones1 = E(nc.sbuf_tensor("ones1", [1, 128], F32))
        smallv = E(nc.sbuf_tensor("smallv", [128, 3 * DT_], F32))
        psA = E(nc.psum_tensor("psA", [128, NCH], F32))
        psR = E(nc.psum_tensor("psR", [1, NCH], F32))
        psB1 = E(nc.psum_tensor("psB1", [128, NCH], F32))
        psB2 = E(nc.psum_tensor("psB2", [128, NCH], F32))
        sem = E(nc.semaphore("gs"))

        S1 = lambda: vs[:, 0:NTOK]
        S2 = lambda: vs[:, NTOK:2 * NTOK]
        MU = lambda: vs[:, 2 * NTOK:3 * NTOK]
        RSTD = lambda: vs[:, 3 * NTOK:4 * NTOK]
        NMR = lambda: vs[:, 4 * NTOK:5 * NTOK]
        SC = lambda: vs[:, 5 * NTOK:6 * NTOK]

        for k in range(DT_):
            sp.ld(xsb[:, k * NTOK:(k + 1) * NTOK], Xin[k * 128:(k + 1) * 128, :])
        sp.ld(dtwsb[:], Wdt[:])
        sp.ld(smallv[:, 0:DT_], Bdt[:])
        sp.ld(smallv[:, DT_:2 * DT_], G1[:])
        sp.ld(smallv[:, 2 * DT_:3 * DT_], WR[:])
        sp.v(lambda: nc.vector.memset(ones128[:], 1.0))
        sp.v(lambda: nc.vector.memset(csb[:, 0:1], LN_EPS))
        sp.v(lambda: nc.vector.memset(csb[:, 1:2], 1.0))
        sp.v(lambda: nc.vector.memset(csb[:, 2:3], -1.0))
        sp.v(lambda: nc.vector.memset(ones1[:], 1.0))

        ln_stats(nc, sp, xsb, NTOK, DT_, psR, S1(), S2(), sq, ones128)
        dinv = 1.0 / D
        sp.v(lambda: nc.vector.tensor_scalar_mul(MU(), S1(), dinv))
        sp.v(lambda: nc.vector.tensor_mul(S1(), MU(), MU()))
        sp.v(lambda: nc.vector.tensor_scalar_mul(S2(), S2(), dinv))
        sp.v(lambda: nc.vector.tensor_sub(S2(), S2(), S1()))
        sp.s(lambda: nc.scalar.activation(out=RSTD(), in_=S2(), func=AF.Sqrt, bias=csb[0:1, 0:1]))
        sp.v(lambda: nc.vector.reciprocal(RSTD(), RSTD()))
        sp.v(lambda: nc.vector.tensor_mul(NMR(), MU(), RSTD()))
        sp.v(lambda: nc.vector.tensor_scalar_mul(NMR(), NMR(), -1.0))

        for ch in range(NTOK // NCH):
            bcast_row(nc, sp, psB1, ones1, RSTD(), ch * NCH, NCH)
            bcast_row(nc, sp, psB2, ones1, NMR(), ch * NCH, NCH)
            for k in range(DT_):
                c0 = k * NTOK + ch * NCH
                sp.v(lambda c0=c0: nc.vector.tensor_mul(t1[:, 0:NCH], xsb[:, c0:c0 + NCH], psB1[:, 0:NCH]))
                sp.v(lambda c0=c0: nc.vector.tensor_add(xnsb[:, c0:c0 + NCH], t1[:, 0:NCH], psB2[:, 0:NCH]))

        ln_stats(nc, sp, xnsb, NTOK, DT_, psR, S1(), S2(), sq, ones128)
        rstd_from_s2(nc, sp, S2(), RSTD(), dinv, csb[0:1, 0:1], NTOK)
        for ch in range(NTOK // NCH):
            bcast_row(nc, sp, psB1, ones1, RSTD(), ch * NCH, NCH)
            for k in range(DT_):
                c0 = k * NTOK + ch * NCH
                sp.v(lambda c0=c0, k=k: nc.vector.scalar_tensor_tensor(
                    r1sb[:, c0:c0 + NCH], xnsb[:, c0:c0 + NCH], smallv[:, DT_ + k:DT_ + k + 1],
                    psB1[:, 0:NCH], ALU.mult, ALU.mult))

        for ch in range(NTOK // NCH):
            for k in range(DT_):
                c0 = k * NTOK + ch * NCH
                sp.t(lambda c0=c0, k=k, st=(k == 0), sp_=(k == DT_ - 1):
                     nc.tensor.matmul(psR[:, 0:NCH], smallv[:, 2 * DT_ + k:2 * DT_ + k + 1],
                                      xnsb[:, c0:c0 + NCH], start=st, stop=sp_))
            sp.v(lambda ch=ch: nc.vector.tensor_copy(SC()[:, ch * NCH:(ch + 1) * NCH], psR[:, 0:NCH]))
        sp.st(Osc[:], SC())

        for k in range(DT_):
            sp.st(Oxn[k * 128:(k + 1) * 128, :], xnsb[:, k * NTOK:(k + 1) * NTOK])
        for k in range(DT_):
            for ch in range(NTOK // NCH):
                c0 = k * NTOK + ch * NCH
                sp.s(lambda c0=c0: nc.scalar.activation(out=outsb[:, 0:NCH], in_=xnsb[:, c0:c0 + NCH], func=AF.Silu))
                sp.st(Ogb[k * 128:(k + 1) * 128, ch * NCH:(ch + 1) * NCH], outsb[:, 0:NCH])

        for m in range(PROJ // 128):
            for k in range(DT_):
                sp.ld(wtile[:, k * 128:(k + 1) * 128], Win[k * 128:(k + 1) * 128, m * 128:(m + 1) * 128])
            for ch in range(NTOK // NCH):
                for k in range(DT_):
                    c0 = k * NTOK + ch * NCH
                    sp.t(lambda c0=c0, k=k, st=(k == 0), sp_=(k == DT_ - 1):
                         nc.tensor.matmul(psA[:, 0:NCH], wtile[:, k * 128:(k + 1) * 128],
                                          r1sb[:, c0:c0 + NCH], start=st, stop=sp_))
                if m == 40:
                    sp.v(lambda ch=ch: nc.vector.tensor_copy(drsb[:, ch * NCH:(ch + 1) * NCH], psA[:, 0:NCH]))
                else:
                    sp.v(lambda: nc.vector.tensor_copy(outsb[:, 0:NCH], psA[:, 0:NCH]))
                    if m < 16:
                        tgt, r0 = Oz, m * 128
                    elif m < 20:
                        tgt, r0 = Oxs, (m - 16) * 128
                    elif m < 24:
                        tgt, r0 = Obc, (m - 20) * 128
                    else:
                        tgt, r0 = Occ, (m - 24) * 128
                    sp.st(tgt[r0:r0 + 128, ch * NCH:(ch + 1) * NCH], outsb[:, 0:NCH])

        for m2 in range(DT_):
            for ch in range(NTOK // NCH):
                sp.t(lambda m2=m2, ch=ch: nc.tensor.matmul(
                    psA[:, 0:NCH], dtwsb[:, m2 * 128:(m2 + 1) * 128],
                    drsb[:, ch * NCH:(ch + 1) * NCH], start=True, stop=True))
                sp.v(lambda m2=m2: nc.vector.tensor_scalar_add(outsb[:, 0:NCH], psA[:, 0:NCH], smallv[:, m2:m2 + 1]))
                sp.s(lambda: nc.scalar.activation(out=t1[:, 0:NCH], in_=outsb[:, 0:NCH], func=AF.Abs))
                sp.s(lambda: nc.scalar.activation(out=t1[:, 0:NCH], in_=t1[:, 0:NCH], func=AF.Exp, scale=csb[:, 2:3]))
                sp.s(lambda: nc.scalar.activation(out=t1[:, 0:NCH], in_=t1[:, 0:NCH], func=AF.Ln, bias=csb[:, 1:2]))
                sp.s(lambda: nc.scalar.activation(out=t2[:, 0:NCH], in_=outsb[:, 0:NCH], func=AF.Relu))
                sp.v(lambda: nc.vector.tensor_add(t2[:, 0:NCH], t2[:, 0:NCH], t1[:, 0:NCH]))
                sp.st(Odt[m2 * 128:(m2 + 1) * 128, ch * NCH:(ch + 1) * NCH], t2[:, 0:NCH])

        sp.build(sem)
    return nc


def prep_A_inputs(inputs):
    x = np.asarray(inputs["L0_attn_hidden_states"], np.float32).reshape(B * T, D)
    w_in = fm(inputs["L0_attn_in_proj_w"])
    w_dt = fm(inputs["L0_attn_dt_proj_w"])
    b_dt = colv(inputs["L0_attn_dt_proj_b"])
    g1 = colv(inputs["L0_attn_norm1_w"])
    wr = colv(inputs["L0_ffn_w_router"])
    maps = []
    for c in range(N_CORES):
        sl = slice(c * NTOK, (c + 1) * NTOK)
        maps.append({"x": fm(x[sl]), "w_in": w_in, "w_dt": w_dt, "b_dt": b_dt,
                     "g1": g1, "wr": wr})
    return maps


def run_A(inputs):
    nc = build_A()
    maps = prep_A_inputs(inputs)
    res = run_bass_kernel_spmd(nc, maps, core_ids=list(range(N_CORES)))
    return res.results


TCB = 256
TCC = 256
NMID = 44

def build_B():
    nc = bass.Bass("TRN2", target_bir_lowering=False, debug=False)
    Idt = nc.dram_tensor("dt", [512, T], F32, kind="ExternalInput").ap()
    Icc = nc.dram_tensor("cc", [512, T], F32, kind="ExternalInput").ap()
    Ixs = nc.dram_tensor("xs", [128, T], F32, kind="ExternalInput").ap()
    Ibc = nc.dram_tensor("bc", [128, T], F32, kind="ExternalInput").ap()
    Ial = nc.dram_tensor("alog", [128, 4 * 32], F32, kind="ExternalInput").ap()
    Idp = nc.dram_tensor("dp", [128, 4], F32, kind="ExternalInput").ap()
    Oy = nc.dram_tensor("y", [512, T], F32, kind="ExternalOutput").ap()

    sp = SP(nc)
    NS = 32
    with ExitStack() as es:
        E = es.enter_context
        dtsb = E(nc.sbuf_tensor("dtsb", [128, T], F32))
        xrsb = E(nc.sbuf_tensor("xrsb", [128, T], F32))
        usb = E(nc.sbuf_tensor("usb", [128, T], F32))
        ysb = E(nc.sbuf_tensor("ysb", [128, T], F32))
        mda = E(nc.sbuf_tensor("mda", [128, NS * TCB], F32))
        ub = E(nc.sbuf_tensor("ub", [128, NS * TCB], F32))
        hb = E(nc.sbuf_tensor("hb", [128, NS * TCB], F32))
        cr = E(nc.sbuf_tensor("cr", [128, NS * TCB], F32))
        asb = E(nc.sbuf_tensor("asb", [128, 4 * NS], F32))
        dpsb = E(nc.sbuf_tensor("dpsb", [128, 4], F32))
        stt = E(nc.sbuf_tensor("stt", [128, NS], F32))
        sem = E(nc.semaphore("gs"))

        sp.ld(asb[:], Ial[:])
        sp.ld(dpsb[:], Idp[:])
        sp.s(lambda: nc.scalar.activation(out=asb[:], in_=asb[:], func=AF.Exp))
        sp.v(lambda: nc.vector.tensor_scalar_mul(asb[:], asb[:], -1.0))

        for j in range(4):
            sp.ld(dtsb[:], Idt[j * 128:(j + 1) * 128, :])
            sp.ld(xrsb[:], Ixs[j * 32:(j + 1) * 32, :].unsqueeze(0).broadcast_to([4, 32, T]))
            sp.v(lambda j=j: nc.vector.tensor_mul(usb[:], dtsb[:], xrsb[:]))
            sp.v(lambda: nc.vector.memset(stt[:], 0.0))
            A_t = asb[:, j * NS:(j + 1) * NS]
            for cch in range(T // TCB):
                t0 = cch * TCB
                sp.v(lambda t0=t0, A_t=A_t: nc.vector.tensor_mul(
                    mda[:].rearrange("p (n t) -> p n t", n=NS),
                    dtsb[:, t0:t0 + TCB].unsqueeze(1).broadcast_to([128, NS, TCB]),
                    A_t.unsqueeze(2).broadcast_to([128, NS, TCB])))
                sp.s(lambda: nc.scalar.activation(out=mda[:], in_=mda[:], func=AF.Exp))
                sp.ld(ub[:], Ibc[j * 32:(j + 1) * 32, t0:t0 + TCB].unsqueeze(0).broadcast_to([128, 32, TCB]))
                sp.v(lambda t0=t0: nc.vector.tensor_mul(
                    ub[:].rearrange("p (n t) -> p n t", n=NS),
                    usb[:, t0:t0 + TCB].unsqueeze(1).broadcast_to([128, NS, TCB]),
                    ub[:].rearrange("p (n t) -> p n t", n=NS)))
                for n in range(NS):
                    sp.v(lambda n=n: nc.vector.tensor_tensor_scan(
                        hb[:, n * TCB:(n + 1) * TCB], mda[:, n * TCB:(n + 1) * TCB],
                        ub[:, n * TCB:(n + 1) * TCB], stt[:, n:n + 1], ALU.mult, ALU.add))
                sp.v(lambda: nc.vector.tensor_copy(stt[:], hb[:, TCB - 1:NS * TCB:TCB]))
                for gq in range(4):
                    sp.ld(cr[gq * 32:(gq + 1) * 32, 0:NS * TCB],
                          Icc[j * 128 + gq * 32:j * 128 + gq * 32 + 32, t0:t0 + TCB]
                          .unsqueeze(0).broadcast_to([32, 32, TCB]))
                sp.v(lambda: nc.vector.tensor_mul(cr[:], hb[:], cr[:]))
                sp.v(lambda t0=t0: nc.vector.reduce_sum(
                    ysb[:, t0:t0 + TCB],
                    cr[:].rearrange("p (n t) -> p t n", n=NS), axis=AX.X))
            sp.v(lambda j=j: nc.vector.scalar_tensor_tensor(
                ysb[:], xrsb[:], dpsb[:, j:j + 1], ysb[:], ALU.mult, ALU.add))
            sp.st(Oy[j * 128:(j + 1) * 128, :], ysb[:])
        sp.build(sem)
    return nc


def prep_B_inputs(resA, inputs):
    alog = np.asarray(inputs["L0_attn_A_log"], np.float32)
    dpar = np.asarray(inputs["L0_attn_D_param"], np.float32)
    maps = []
    for b in range(B):
        dt_f = np.concatenate([resA[4 * b + i]["dt"] for i in range(4)], axis=1)
        cc_f = np.concatenate([resA[4 * b + i]["cc"] for i in range(4)], axis=1)
        xs_f = np.concatenate([resA[4 * b + i]["xs"] for i in range(4)], axis=1)
        bc_f = np.concatenate([resA[4 * b + i]["bc"] for i in range(4)], axis=1)
        for a in range(4):
            r0 = 512 * a
            al = alog[r0:r0 + 512].reshape(4, 128, 32).transpose(1, 0, 2).reshape(128, 128)
            maps.append({
                "dt": np.ascontiguousarray(dt_f[r0:r0 + 512]),
                "cc": np.ascontiguousarray(cc_f[r0:r0 + 512]),
                "xs": np.ascontiguousarray(xs_f[128 * a:128 * a + 128]),
                "bc": np.ascontiguousarray(bc_f[128 * a:128 * a + 128]),
                "alog": np.ascontiguousarray(al),
                "dp": colv(dpar[r0:r0 + 512], 4),
            })
    return maps


def band_ld(sp, nc, wtile, W, m, nk, parts=2):
    K = nk * 128
    step = nk // parts
    for p0 in range(0, nk, step):
        sp.ld(wtile[:, p0 * 128:(p0 + step) * 128],
              W[p0 * 128:(p0 + step) * 128, m * 128:(m + 1) * 128]
              .rearrange("(k p) j -> p k j", p=128))


def build_C():
    nc = bass.Bass("TRN2", target_bir_lowering=False, debug=False)
    Iy = nc.dram_tensor("y", [D, NTOK], F32, kind="ExternalInput").ap()
    Iz = nc.dram_tensor("z", [D, NTOK], F32, kind="ExternalInput").ap()
    Ixn = nc.dram_tensor("xn0", [D, NTOK], F32, kind="ExternalInput").ap()
    Wo = nc.dram_tensor("w_out", [D, D], F32, kind="ExternalInput").ap()
    Wg = nc.dram_tensor("w_gate", [D, 5632], F32, kind="ExternalInput").ap()
    Wu = nc.dram_tensor("w_up", [D, 5632], F32, kind="ExternalInput").ap()
    Wd = nc.dram_tensor("w_down", [5632, D], F32, kind="ExternalInput").ap()
    G2 = nc.dram_tensor("g2", [128, DT_], F32, kind="ExternalInput").ap()
    Os0 = nc.dram_tensor("s0", [D, NTOK], F32, kind="ExternalOutput").ap()

    sp = SP(nc)
    with ExitStack() as es:
        E = es.enter_context
        ybuf = E(nc.sbuf_tensor("ybuf", [128, DT_ * TCC], F32))
        zbuf = E(nc.sbuf_tensor("zbuf", [128, DT_ * TCC], F32))
        h1buf = E(nc.sbuf_tensor("h1buf", [128, DT_ * TCC], F32))
        midbuf = E(nc.sbuf_tensor("midbuf", [128, NMID * TCC], F32))
        wtile = E(nc.sbuf_tensor("wtile", [128, NMID * 128], F32))
        xnsc = E(nc.sbuf_tensor("xnsc", [128, TCC], F32))
        gsc = E(nc.sbuf_tensor("gsc", [128, TCC], F32))
        osc = E(nc.sbuf_tensor("osc", [128, TCC], F32))
        sq = E(nc.sbuf_tensor("sqc", [128, TCC], F32))
        vs = E(nc.sbuf_tensor("vsc", [1, TCC * 4], F32))
        ones128 = E(nc.sbuf_tensor("o128", [128, 1], F32))
        ones1 = E(nc.sbuf_tensor("o1", [1, 128], F32))
        csb = E(nc.sbuf_tensor("csbc", [128, 1], F32))
        g2sb = E(nc.sbuf_tensor("g2sb", [128, DT_], F32))
        psA = E(nc.psum_tensor("psA", [128, NCH], F32))
        psR = E(nc.psum_tensor("psR", [1, NCH], F32))
        psB1 = E(nc.psum_tensor("psB1", [128, NCH], F32))
        sem = E(nc.semaphore("gs"))

        S1 = lambda: vs[:, 0:TCC]
        S2 = lambda: vs[:, TCC:2 * TCC]
        RSTD = lambda: vs[:, 2 * TCC:3 * TCC]

        sp.v(lambda: nc.vector.memset(ones128[:], 1.0))
        sp.v(lambda: nc.vector.memset(ones1[:], 1.0))
        sp.v(lambda: nc.vector.memset(csb[:, 0:1], L0_EPS))
        sp.ld(g2sb[:], G2[:])

        for ch in range(NTOK // TCC):
            t0 = ch * TCC
            for k in range(DT_):
                sp.ld(ybuf[:, k * TCC:(k + 1) * TCC], Iy[k * 128:(k + 1) * 128, t0:t0 + TCC])
                sp.ld(zbuf[:, k * TCC:(k + 1) * TCC], Iz[k * 128:(k + 1) * 128, t0:t0 + TCC])
            for k in range(DT_):
                c0 = k * TCC
                sp.s(lambda c0=c0: nc.scalar.activation(out=zbuf[:, c0:c0 + TCC], in_=zbuf[:, c0:c0 + TCC], func=AF.Silu))
                sp.v(lambda c0=c0: nc.vector.tensor_mul(ybuf[:, c0:c0 + TCC], ybuf[:, c0:c0 + TCC], zbuf[:, c0:c0 + TCC]))
            for m in range(DT_):
                band_ld(sp, nc, wtile, Wo, m, DT_, parts=2)
                for k in range(DT_):
                    sp.t(lambda k=k, st=(k == 0), sp_=(k == DT_ - 1):
                         nc.tensor.matmul(psA[:, 0:TCC], wtile[:, k * 128:(k + 1) * 128],
                                          ybuf[:, k * TCC:(k + 1) * TCC], start=st, stop=sp_))
                sp.ld(xnsc[:], Ixn[m * 128:(m + 1) * 128, t0:t0 + TCC])
                sp.v(lambda m=m: nc.vector.tensor_add(h1buf[:, m * TCC:(m + 1) * TCC], psA[:, 0:TCC], xnsc[:]))
            ln_stats2(nc, sp, h1buf, TCC, DT_, psR, S1(), S2(), sq, ones128)
            rstd_from_s2(nc, sp, S2(), RSTD(), 1.0 / D, csb[0:1, 0:1], TCC)
            sp.t(lambda: nc.tensor.matmul(psB1[:, 0:TCC], ones1[:, 0:128], RSTD()[:, 0:TCC], start=True, stop=True))
            for k in range(DT_):
                c0 = k * TCC
                sp.v(lambda c0=c0, k=k: nc.vector.scalar_tensor_tensor(
                    zbuf[:, c0:c0 + TCC], h1buf[:, c0:c0 + TCC], g2sb[:, k:k + 1],
                    psB1[:, 0:TCC], ALU.mult, ALU.mult))
            for m in range(NMID):
                band_ld(sp, nc, wtile, Wg, m, DT_, parts=2)
                for k in range(DT_):
                    sp.t(lambda k=k, st=(k == 0), sp_=(k == DT_ - 1):
                         nc.tensor.matmul(psA[:, 0:TCC], wtile[:, k * 128:(k + 1) * 128],
                                          zbuf[:, k * TCC:(k + 1) * TCC], start=st, stop=sp_))
                sp.s(lambda: nc.scalar.activation(out=gsc[:], in_=psA[:, 0:TCC], func=AF.Silu))
                band_ld(sp, nc, wtile, Wu, m, DT_, parts=2)
                for k in range(DT_):
                    sp.t(lambda k=k, st=(k == 0), sp_=(k == DT_ - 1):
                         nc.tensor.matmul(psA[:, 0:TCC], wtile[:, k * 128:(k + 1) * 128],
                                          zbuf[:, k * TCC:(k + 1) * TCC], start=st, stop=sp_))
                sp.v(lambda m=m: nc.vector.tensor_mul(midbuf[:, m * TCC:(m + 1) * TCC], gsc[:], psA[:, 0:TCC]))
            for mo in range(DT_):
                band_ld(sp, nc, wtile, Wd, mo, NMID, parts=4)
                for k in range(NMID):
                    sp.t(lambda k=k, st=(k == 0), sp_=(k == NMID - 1):
                         nc.tensor.matmul(psA[:, 0:TCC], wtile[:, k * 128:(k + 1) * 128],
                                          midbuf[:, k * TCC:(k + 1) * TCC], start=st, stop=sp_))
                sp.v(lambda mo=mo: nc.vector.tensor_add(osc[:], psA[:, 0:TCC], h1buf[:, mo * TCC:(mo + 1) * TCC]))
                sp.st(Os0[mo * 128:(mo + 1) * 128, t0:t0 + TCC], osc[:])
        sp.build(sem)
    return nc


def ln_stats2(nc, sp, src_sb, ntok, nt, psR, s1, s2, sq, ones128, base=0, stride=None):
    if stride is None:
        stride = ntok
    for k in range(nt):
        c0 = base + k * stride
        sp.t(lambda c0=c0, st=(k == 0), sp_=(k == nt - 1):
             nc.tensor.matmul(psR[:, 0:ntok], ones128[:, 0:1], src_sb[:, c0:c0 + ntok], start=st, stop=sp_))
    sp.v(lambda: nc.vector.tensor_copy(s1[:, 0:ntok], psR[:, 0:ntok]))
    for k in range(nt):
        c0 = base + k * stride
        sp.s(lambda c0=c0: nc.scalar.activation(out=sq[:, 0:ntok], in_=src_sb[:, c0:c0 + ntok], func=AF.Square))
        sp.t(lambda st=(k == 0), sp_=(k == nt - 1):
             nc.tensor.matmul(psR[:, 0:ntok], ones128[:, 0:1], sq[:, 0:ntok], start=st, stop=sp_))
    sp.v(lambda: nc.vector.tensor_copy(s2[:, 0:ntok], psR[:, 0:ntok]))


def prep_C_inputs(resA, resB, inputs):
    w_out = fm(inputs["L0_attn_out_proj_w"])
    w_g = fm(inputs["L0_attn_gate_w"])
    w_u = fm(inputs["L0_attn_up_w"])
    w_d = fm(inputs["L0_attn_down_w"])
    g2 = colv(inputs["L0_attn_norm2_w"])
    maps = []
    for c in range(N_CORES):
        b, a = c // 4, c % 4
        y_f = np.concatenate([resB[4 * b + i]["y"] for i in range(4)], axis=0)
        maps.append({"y": np.ascontiguousarray(y_f[:, NTOK * a:NTOK * (a + 1)]),
                     "z": resA[c]["z"], "xn0": resA[c]["xn0"],
                     "w_out": w_out, "w_gate": w_g, "w_up": w_u, "w_down": w_d, "g2": g2})
    return maps



def build_D():
    nc = bass.Bass("TRN2", target_bir_lowering=False, debug=False)
    Isc = nc.dram_tensor("sc", [128, 64], F32, kind="ExternalInput").ap()
    Omask = nc.dram_tensor("mask", [128, 64], F32, kind="ExternalOutput").ap()
    Oslot = nc.dram_tensor("slot", [128, 64], F32, kind="ExternalOutput").ap()
    Odbg = nc.dram_tensor("dbg", [1, 32], F32, kind="ExternalOutput").ap()
    Oge = nc.dram_tensor("gedbg", [128, 64], F32, kind="ExternalOutput").ap()

    sp = SP(nc)
    BIG = 1e30
    with ExitStack() as es:
        E = es.enter_context
        s = E(nc.sbuf_tensor("s", [128, 64], F32))
        ge = E(nc.sbuf_tensor("ge", [128, 64], F32))
        sc1 = E(nc.sbuf_tensor("sc1", [128, 64], F32))
        sc2 = E(nc.sbuf_tensor("sc2", [128, 64], F32))
        rs = E(nc.sbuf_tensor("rs", [128, 2], F32))
        ones128 = E(nc.sbuf_tensor("o128", [128, 1], F32))
        ones64f = E(nc.sbuf_tensor("o64f", [128, 64], F32))
        idf = E(nc.sbuf_tensor("idf", [128, 128], F32))
        ltri = E(nc.sbuf_tensor("ltri", [128, 128], F32))
        pii = E(nc.sbuf_tensor("pii", [128, 1], I32))
        fii = E(nc.sbuf_tensor("fii", [128, 128], I32))
        pif = E(nc.sbuf_tensor("pif", [128, 1], F32))
        fif = E(nc.sbuf_tensor("fif", [128, 128], F32))
        vv = E(nc.sbuf_tensor("vv", [1, 32], F32))
        tp = E(nc.sbuf_tensor("tp", [2, 128], F32))
        psB = E(nc.psum_tensor("psB", [128, 64], F32))
        psC = E(nc.psum_tensor("psC", [1, 64], F32))
        psT = E(nc.psum_tensor("psT", [128, 128], F32))
        psO = E(nc.psum_tensor("psO", [128, 2], F32))
        sem = E(nc.semaphore("gs"))

        LO = vv[:, 0:2]; HI = vv[:, 2:4]; MID = vv[:, 4:6]; CNT = vv[:, 6:8]
        OK = vv[:, 8:10]; OKC = vv[:, 10:12]; TMP = vv[:, 12:14]; VV = vv[:, 14:16]
        NEED = vv[:, 16:18]; C512 = vv[:, 18:20]; T2 = vv[:, 20:22]

        sp.ld(s[:], Isc[:])
        sp.v(lambda: nc.vector.memset(ones128[:], 1.0))
        sp.v(lambda: nc.vector.memset(ones64f[:], 1.0))
        sp.v(lambda: nc.vector.memset(vv[:], 0.0))
        sp.v(lambda: nc.vector.memset(C512, 512.0))

        sp.g(lambda: nc.gpsimd.iota(pii[:], pattern=[[0, 1]], base=0, channel_multiplier=1))
        sp.g(lambda: nc.gpsimd.iota(fii[:], pattern=[[1, 128]], base=0, channel_multiplier=0))
        sp.v(lambda: nc.vector.tensor_copy(pif[:], pii[:]))
        sp.v(lambda: nc.vector.tensor_copy(fif[:], fii[:]))
        sp.v(lambda: nc.vector.tensor_scalar(idf[:], fif[:], pif[:], None, ALU.is_equal))
        sp.v(lambda: nc.vector.tensor_scalar(ltri[:], fif[:], pif[:], None, ALU.is_gt))
        INIT_MINMAX = True

        s3 = s[:].rearrange("p (b j) -> p b j", b=2)
        ge3 = ge[:].rearrange("p (b j) -> p b j", b=2)

        def cross_min(dst, src3, op):
            sp.v(lambda: nc.vector.tensor_reduce(rs[:], src3, axis=AX.X, op=op))
            sp.t(lambda: nc.tensor.transpose(psT[0:2, 0:128], rs[:], idf[:]))
            sp.v(lambda: nc.vector.tensor_copy(tp[:], psT[0:2, 0:128]))
            sp.v(lambda: nc.vector.tensor_reduce(tp[0:2, 0:1], tp[0:2, :], axis=AX.X, op=op))
            sp.t(lambda: nc.tensor.transpose(psT[0:1, 0:2], tp[0:2, 0:1], idf[0:2, 0:2]))
            sp.v(lambda d=dst: nc.vector.tensor_copy(d, psT[0:1, 0:2]))

        def count_ge(dst_cnt, thr):
            sp.t(lambda: nc.tensor.matmul(psB[:, 0:2], ones1x(nc)[:, 0:128], thr, start=True, stop=True))
            sp.v(lambda: nc.vector.tensor_tensor(ge3, s3, psB[:, 0:2].unsqueeze(2).broadcast_to([128, 2, 32]), ALU.is_ge))
            sp.v(lambda: nc.vector.reduce_sum(rs[:], ge3, axis=AX.X))
            sp.t(lambda: nc.tensor.matmul(psC[:, 0:2], ones128[:, 0:1], rs[:], start=True, stop=True))
            sp.v(lambda d=dst_cnt: nc.vector.tensor_copy(d, psC[:, 0:2]))

        _o1 = {}
        def ones1x(nc):
            return _o1["t"]
        _o1["t"] = E(nc.sbuf_tensor("o1", [1, 128], F32))
        sp.v(lambda: nc.vector.memset(_o1["t"][:], 1.0))

        cross_min(LO, s3, ALU.min)
        cross_min(HI, s3, ALU.max)
        sp.v(lambda: nc.vector.tensor_scalar_add(LO, LO, -1.0))
        sp.v(lambda: nc.vector.tensor_scalar_add(HI, HI, 1.0))
        for it in range(44):
            sp.v(lambda: nc.vector.tensor_add(MID, LO, HI))
            sp.v(lambda: nc.vector.tensor_scalar_mul(MID, MID, 0.5))
            count_ge(CNT, MID)
            sp.v(lambda: nc.vector.tensor_tensor(OK, CNT, C512, ALU.is_ge))
            sp.v(lambda: nc.vector.tensor_scalar(OKC, OK, -1.0, 1.0, ALU.mult, ALU.add))
            sp.v(lambda: nc.vector.tensor_sub(TMP, MID, LO))
            sp.v(lambda: nc.vector.tensor_mul(TMP, TMP, OK))
            sp.v(lambda: nc.vector.tensor_add(LO, LO, TMP))
            sp.v(lambda: nc.vector.tensor_sub(TMP, MID, HI))
            sp.v(lambda: nc.vector.tensor_mul(TMP, TMP, OKC))
            sp.v(lambda: nc.vector.tensor_add(HI, HI, TMP))
        count_ge(CNT, LO)
        sp.st(Odbg[:], vv[:, 0:32])
        sp.st(Oge[:], ge[:])
        sp.v(lambda: nc.vector.tensor_mul(sc1[:], s[:], ge[:]))
        sp.v(lambda: nc.vector.tensor_scalar(sc2[:], ge[:], -BIG, BIG, ALU.mult, ALU.add))
        sp.v(lambda: nc.vector.tensor_add(sc1[:], sc1[:], sc2[:]))
        cross_min(VV, sc1[:].rearrange("p (b j) -> p b j", b=2), ALU.min)
        sp.t(lambda: nc.tensor.matmul(psB[:, 0:2], ones1x(nc)[:, 0:128], VV, start=True, stop=True))
        sp.v(lambda: nc.vector.tensor_tensor(sc1[:].rearrange("p (b j) -> p b j", b=2), s3,
                                             psB[:, 0:2].unsqueeze(2).broadcast_to([128, 2, 32]), ALU.is_gt))
        sp.v(lambda: nc.vector.tensor_tensor(sc2[:].rearrange("p (b j) -> p b j", b=2), s3,
                                             psB[:, 0:2].unsqueeze(2).broadcast_to([128, 2, 32]), ALU.is_equal))
        sp.v(lambda: nc.vector.reduce_sum(rs[:], sc1[:].rearrange("p (b j) -> p b j", b=2), axis=AX.X))
        sp.t(lambda: nc.tensor.matmul(psC[:, 0:2], ones128[:, 0:1], rs[:], start=True, stop=True))
        sp.v(lambda: nc.vector.tensor_copy(NEED, psC[:, 0:2]))
        sp.v(lambda: nc.vector.tensor_sub(NEED, C512, NEED))

        def pos_cumsum(src, dst):
            for b_ in range(2):
                sp.v(lambda b_=b_: nc.vector.tensor_tensor_scan(
                    dst[:, b_ * 32:(b_ + 1) * 32], ones64f[:, 0:32], src[:, b_ * 32:(b_ + 1) * 32],
                    0.0, ALU.mult, ALU.add))
            sp.v(lambda: nc.vector.tensor_copy(rs[:], dst[:, 31:64:32]))
            sp.t(lambda: nc.tensor.matmul(psO[:, 0:2], ltri[:], rs[:], start=True, stop=True))
            sp.v(lambda: nc.vector.tensor_tensor(dst[:].rearrange("p (b j) -> p b j", b=2),
                                                 dst[:].rearrange("p (b j) -> p b j", b=2),
                                                 psO[:, 0:2].unsqueeze(2).broadcast_to([128, 2, 32]), ALU.add))

        pos_cumsum(sc2, ge)
        sp.t(lambda: nc.tensor.matmul(psB[:, 0:2], ones1x(nc)[:, 0:128], NEED, start=True, stop=True))
        sp.v(lambda: nc.vector.tensor_tensor(ge3, ge3, psB[:, 0:2].unsqueeze(2).broadcast_to([128, 2, 32]), ALU.is_le))
        sp.v(lambda: nc.vector.tensor_mul(ge[:], ge[:], sc2[:]))
        sp.v(lambda: nc.vector.tensor_add(sc1[:], sc1[:], ge[:]))
        sp.st(Omask[:], sc1[:])
        pos_cumsum(sc1, sc2)
        sp.v(lambda: nc.vector.tensor_scalar_add(sc2[:, 0:32], sc2[:, 0:32], -1.0))
        sp.v(lambda: nc.vector.tensor_scalar_add(sc2[:, 32:64], sc2[:, 32:64], 511.0))
        sp.st(Oslot[:], sc2[:])
        sp.build(sem)
    return nc


ROPE_C1 = float(np.float32(2 * np.pi))
ROPE_C2 = float(np.float64(2 * np.pi) - np.float32(2 * np.pi))

def build_E():
    nc = bass.Bass("TRN2", target_bir_lowering=False, debug=False)
    S = 1024
    Isel = nc.dram_tensor("sel", [D, S], F32, kind="ExternalInput").ap()
    Iselq = nc.dram_tensor("selq", [D, 128], F32, kind="ExternalInput").ap()
    Iposq = nc.dram_tensor("posq", [128, 1], F32, kind="ExternalInput").ap()
    Iposk = nc.dram_tensor("posk", [128, 8], F32, kind="ExternalInput").ap()
    Iscq = nc.dram_tensor("scq", [1, 128], F32, kind="ExternalInput").ap()
    Icm = nc.dram_tensor("cmask", [128, 8 * 128], F32, kind="ExternalInput").ap()
    Wq = nc.dram_tensor("wq", [D, D], F32, kind="ExternalInput").ap()
    Wk = nc.dram_tensor("wk", [D, 1024], F32, kind="ExternalInput").ap()
    Wv = nc.dram_tensor("wv", [D, 1024], F32, kind="ExternalInput").ap()
    Wo = nc.dram_tensor("wo", [D, D], F32, kind="ExternalInput").ap()
    Wg = nc.dram_tensor("wg", [D, 8192], F32, kind="ExternalInput").ap()
    Wu = nc.dram_tensor("wu", [D, 8192], F32, kind="ExternalInput").ap()
    Wd = nc.dram_tensor("wd", [8192, D], F32, kind="ExternalInput").ap()
    Vln1 = nc.dram_tensor("ln1", [128, DT_], F32, kind="ExternalInput").ap()
    Vln2 = nc.dram_tensor("ln2", [128, DT_], F32, kind="ExternalInput").ap()
    Vbq = nc.dram_tensor("bq", [128, DT_], F32, kind="ExternalInput").ap()
    Vbk = nc.dram_tensor("bk", [128, 8], F32, kind="ExternalInput").ap()
    Vbv = nc.dram_tensor("bv", [128, 8], F32, kind="ExternalInput").ap()
    Og = nc.dram_tensor("gated", [D, 128], F32, kind="ExternalOutput").ap()

    sp = SP(nc)
    NH, NKV, HD = 16, 8, 128
    with ExitStack() as es:
        E = es.enter_context
        bufA = E(nc.sbuf_tensor("bufA", [128, DT_ * S], F32))
        bufH = E(nc.sbuf_tensor("bufH", [128, DT_ * S], F32))
        qsb = E(nc.sbuf_tensor("qsb", [128, DT_ * 128], F32))
        selmy = E(nc.sbuf_tensor("selmy", [128, DT_ * 128], F32))
        x1sb = E(nc.sbuf_tensor("x1sb", [128, DT_ * 128], F32))
        avfm = E(nc.sbuf_tensor("avfm", [128, DT_ * 128], F32))
        h2sb = qsb
        cmsb = E(nc.sbuf_tensor("cmsb", [128, 8 * 128], F32))
        wtile = E(nc.sbuf_tensor("wtileE", [128, 32 * 128], F32))
        trig = E(nc.sbuf_tensor("trig", [128, 9 * 128], F32))
        tmq = E(nc.sbuf_tensor("tmq", [128, 128], F32))
        tmq2 = E(nc.sbuf_tensor("tmq2", [128, 128], F32))
        expb = E(nc.sbuf_tensor("expb", [128, 128], F32))
        gsc = E(nc.sbuf_tensor("gscE", [128, 128], F32))
        sq = E(nc.sbuf_tensor("sqE", [128, 512], F32))
        idf = E(nc.sbuf_tensor("idfE", [128, 128], F32))
        pii = E(nc.sbuf_tensor("piiE", [128, 1], I32))
        fii = E(nc.sbuf_tensor("fiiE", [128, 128], I32))
        nint = E(nc.sbuf_tensor("nintE", [128, 64], I32))
        angs = E(nc.sbuf_tensor("angsE", [128, 3 * 64], F32))
        vecs = E(nc.sbuf_tensor("vecsE", [1, 3 * S + 384], F32))
        smv = E(nc.sbuf_tensor("smvE", [128, 5 * DT_], F32))
        csb = E(nc.sbuf_tensor("csbE", [128, 4], F32))
        ones128 = E(nc.sbuf_tensor("o128E", [128, 1], F32))
        ones1 = E(nc.sbuf_tensor("o1E", [1, 128], F32))
        ifr = E(nc.sbuf_tensor("ifrE", [1, 64], F32))
        ifri = E(nc.sbuf_tensor("ifriE", [1, 64], I32))
        psA = E(nc.psum_tensor("psAE", [128, NCH], F32))
        psS = E(nc.psum_tensor("psSE", [128, 128], F32))
        psAV = E(nc.psum_tensor("psAVE", [128, 128], F32))
        psSum = E(nc.psum_tensor("psSumE", [1, 128], F32))
        psB = E(nc.psum_tensor("psBE", [128, 128], F32))
        psR = E(nc.psum_tensor("psRE", [1, NCH], F32))
        psT = E(nc.psum_tensor("psTE", [128, 128], F32))
        sem = E(nc.semaphore("gs"))

        RSTD = lambda: vecs[:, 0:S]
        S1 = lambda: vecs[:, S:2 * S]
        S2 = lambda: vecs[:, 2 * S:3 * S]
        AUX = lambda: vecs[:, 3 * S:3 * S + 384]

        for k in range(DT_):
            sp.ld(bufA[:, k * S:(k + 1) * S], Isel[k * 128:(k + 1) * 128, :])
            sp.ld(selmy[:, k * 128:(k + 1) * 128], Iselq[k * 128:(k + 1) * 128, :])
        sp.ld(cmsb[:], Icm[:])
        sp.ld(smv[:, 0:DT_], Vln1[:])
        sp.ld(smv[:, DT_:2 * DT_], Vln2[:])
        sp.ld(smv[:, 2 * DT_:3 * DT_], Vbq[:])
        sp.ld(smv[:, 3 * DT_:3 * DT_ + 8], Vbk[:])
        sp.ld(smv[:, 4 * DT_:4 * DT_ + 8], Vbv[:])
        sp.v(lambda: nc.vector.memset(ones128[:], 1.0))
        sp.v(lambda: nc.vector.memset(ones1[:], 1.0))
        sp.v(lambda: nc.vector.memset(csb[:, 0:1], MOD_EPS))
        sp.v(lambda: nc.vector.memset(csb[:, 1:2], -np.log(10000.0) / 64.0))
        sp.v(lambda: nc.vector.memset(csb[:, 2:3], 1.0))
        sp.g(lambda: nc.gpsimd.iota(pii[:], pattern=[[0, 1]], base=0, channel_multiplier=1))
        sp.g(lambda: nc.gpsimd.iota(fii[:], pattern=[[1, 128]], base=0, channel_multiplier=0))
        sp.v(lambda: nc.vector.tensor_copy(idf[:], fii[:]))
        sp.v(lambda: nc.vector.tensor_copy(angs[:, 0:1], pii[:]))
        sp.v(lambda: nc.vector.tensor_scalar(idf[:], idf[:], angs[:, 0:1], None, ALU.is_equal))
        sp.g(lambda: nc.gpsimd.iota(ifri[:], pattern=[[1, 64]], base=0, channel_multiplier=0))
        sp.v(lambda: nc.vector.tensor_copy(ifr[:], ifri[:]))
        sp.s(lambda: nc.scalar.activation(out=ifr[:], in_=ifr[:], func=AF.Exp, scale=csb[0:1, 1:2]))

        def rope_trig(pos_col, dstc):
            sp.t(lambda: nc.tensor.matmul(psB[:, 0:64], ones1[:, 0:128], ifr[:], start=True, stop=True))
            A_ = angs[:, 0:64]; R_ = angs[:, 64:128]; W_ = angs[:, 128:192]
            sp.v(lambda: nc.vector.tensor_scalar(A_, psB[:, 0:64], pos_col, None, ALU.mult))
            for (dst, phase) in ((dstc + 64, 0.0), (dstc, np.pi / 2)):
                sp.v(lambda phase=phase: nc.vector.tensor_scalar(W_, A_, 1.0 / ROPE_C1, phase / ROPE_C1, ALU.mult, ALU.add))
                sp.v(lambda: nc.vector.tensor_copy(nint[:], W_))
                sp.v(lambda: nc.vector.tensor_copy(W_, nint[:]))
                sp.v(lambda phase=phase: nc.vector.tensor_scalar(R_, A_, 1.0, phase, ALU.mult, ALU.add))
                sp.v(lambda: nc.vector.scalar_tensor_tensor(R_, W_, -ROPE_C1, R_, ALU.mult, ALU.add))
                sp.v(lambda: nc.vector.scalar_tensor_tensor(R_, W_, -ROPE_C2, R_, ALU.mult, ALU.add))
                sp.s(lambda dst=dst: nc.scalar.activation(out=trig[:, dst:dst + 64], in_=R_, func=AF.Sin))

        def rope_tile(fm_ap, trigc):
            sp.t(lambda: nc.tensor.transpose(psT[:], fm_ap, idf[:]))
            sp.v(lambda: nc.vector.tensor_copy(tmq[:], psT[:]))
            C_ = trig[:, trigc:trigc + 64]; Sn = trig[:, trigc + 64:trigc + 128]
            sp.v(lambda: nc.vector.tensor_mul(tmq2[:, 0:64], tmq[:, 0:64], C_))
            sp.v(lambda: nc.vector.tensor_mul(gsc[:, 0:64], tmq[:, 64:128], Sn))
            sp.v(lambda: nc.vector.tensor_sub(tmq2[:, 0:64], tmq2[:, 0:64], gsc[:, 0:64]))
            sp.v(lambda: nc.vector.tensor_mul(tmq2[:, 64:128], tmq[:, 64:128], C_))
            sp.v(lambda: nc.vector.tensor_mul(gsc[:, 0:64], tmq[:, 0:64], Sn))
            sp.v(lambda: nc.vector.tensor_add(tmq2[:, 64:128], tmq2[:, 64:128], gsc[:, 0:64]))
            sp.t(lambda: nc.tensor.transpose(psT[:], tmq2[:], idf[:]))
            sp.v(lambda fm_ap=fm_ap: nc.vector.tensor_copy(fm_ap, psT[:]))

        for half in range(2):
            ln_stats2(nc, sp, bufA, 512, DT_, psR, S1()[:, half * 512:(half + 1) * 512],
                      S2()[:, half * 512:(half + 1) * 512], sq, ones128, base=half * 512, stride=S)
        sp.v(lambda: nc.vector.tensor_scalar_mul(S2(), S2(), 1.0 / D))
        sp.s(lambda: nc.scalar.activation(out=RSTD(), in_=S2(), func=AF.Sqrt, bias=csb[0:1, 0:1]))
        sp.v(lambda: nc.vector.reciprocal(RSTD(), RSTD()))
        for half in range(2):
            sp.t(lambda half=half: nc.tensor.matmul(psA[:, 0:512], ones1[:, 0:128],
                                                    RSTD()[:, half * 512:(half + 1) * 512], start=True, stop=True))
            for k in range(DT_):
                c0 = k * S + half * 512
                sp.v(lambda c0=c0, k=k: nc.vector.scalar_tensor_tensor(
                    bufH[:, c0:c0 + 512], bufA[:, c0:c0 + 512], smv[:, k:k + 1], psA[:, 0:512], ALU.mult, ALU.mult))
        ln_stats2(nc, sp, selmy, 128, DT_, psR, S1()[:, 0:128], S2()[:, 0:128], sq, ones128)
        sp.v(lambda: nc.vector.tensor_scalar_mul(S2()[:, 0:128], S2()[:, 0:128], 1.0 / D))
        sp.s(lambda: nc.scalar.activation(out=AUX()[:, 0:128], in_=S2()[:, 0:128], func=AF.Sqrt, bias=csb[0:1, 0:1]))
        sp.v(lambda: nc.vector.reciprocal(AUX()[:, 0:128], AUX()[:, 0:128]))
        sp.t(lambda: nc.tensor.matmul(psB[:], ones1[:, 0:128], AUX()[:, 0:128], start=True, stop=True))
        for k in range(DT_):
            sp.v(lambda k=k: nc.vector.scalar_tensor_tensor(
                x1sb[:, k * 128:(k + 1) * 128], selmy[:, k * 128:(k + 1) * 128], smv[:, k:k + 1],
                psB[:], ALU.mult, ALU.mult))
        for m in range(DT_):
            band_ld(sp, nc, wtile, Wq, m, DT_, parts=2)
            for k in range(DT_):
                sp.t(lambda k=k, st=(k == 0), sp_=(k == DT_ - 1):
                     nc.tensor.matmul(psS[:], wtile[:, k * 128:(k + 1) * 128],
                                      x1sb[:, k * 128:(k + 1) * 128], start=st, stop=sp_))
            sp.v(lambda m=m: nc.vector.tensor_scalar_add(qsb[:, m * 128:(m + 1) * 128], psS[:], smv[:, 2 * DT_ + m:2 * DT_ + m + 1]))
        for (W_, bcol, off) in ((Wk, 3 * DT_, 0), (Wv, 4 * DT_, 8 * S)):
            for m in range(8):
                band_ld(sp, nc, wtile, W_, m, DT_, parts=2)
                for half in range(2):
                    for k in range(DT_):
                        sp.t(lambda k=k, half=half, st=(k == 0), sp_=(k == DT_ - 1):
                             nc.tensor.matmul(psA[:, 0:512], wtile[:, k * 128:(k + 1) * 128],
                                              bufH[:, k * S + half * 512:k * S + half * 512 + 512], start=st, stop=sp_))
                    sp.v(lambda m=m, half=half, bcol=bcol, off=off: nc.vector.tensor_scalar_add(
                        bufA[:, off + m * S + half * 512:off + m * S + half * 512 + 512], psA[:, 0:512],
                        smv[:, bcol + m:bcol + m + 1]))
        posq_sb = E(nc.sbuf_tensor("posqE", [128, 1], F32))
        posk_sb = E(nc.sbuf_tensor("poskE", [128, 8], F32))
        sp.ld(posq_sb[:], Iposq[:])
        sp.ld(posk_sb[:], Iposk[:])
        scq_sb = E(nc.sbuf_tensor("scqE", [1, 128], F32))
        sp.ld(scq_sb[:], Iscq[:])
        rope_trig(posq_sb[:, 0:1], 8 * 128)
        for h in range(NH):
            rope_tile(qsb[:, h * 128:(h + 1) * 128], 8 * 128)
        for st_ in range(8):
            rope_trig(posk_sb[:, st_:st_ + 1], st_ * 128)
        for kh in range(NKV):
            for st_ in range(8):
                rope_tile(bufA[:, kh * S + st_ * 128:kh * S + st_ * 128 + 128], st_ * 128)
        for kh in range(NKV):
            for st_ in range(8):
                sp.t(lambda kh=kh, st_=st_: nc.tensor.transpose(
                    psT[:], bufA[:, 8 * S + kh * S + st_ * 128:8 * S + kh * S + st_ * 128 + 128], idf[:]))
                sp.v(lambda kh=kh, st_=st_: nc.vector.tensor_copy(
                    bufH[:, (kh * 8 + st_) * 128:(kh * 8 + st_ + 1) * 128], psT[:]))
        for kh in range(NKV):
            for st_ in range(8):
                sp.s(lambda kh=kh, st_=st_: nc.scalar.activation(
                    out=sq[:, 0:128], in_=bufA[:, kh * S + st_ * 128:kh * S + st_ * 128 + 128], func=AF.Square))
                sp.t(lambda st_=st_: nc.tensor.matmul(psSum[:], ones128[:, 0:1], sq[:, 0:128], start=True, stop=True))
                sp.v(lambda st_=st_: nc.vector.tensor_copy(S1()[:, st_ * 128:(st_ + 1) * 128], psSum[:]))
            sp.v(lambda kh=kh: nc.vector.reduce_max(AUX()[:, 256 + kh:256 + kh + 1], S1()[:, 0:S], axis=AX.X))
        for h in range(NH):
            kh = h // 2
            sp.s(lambda h=h: nc.scalar.activation(out=sq[:, 0:128], in_=qsb[:, h * 128:(h + 1) * 128], func=AF.Square))
            sp.t(lambda: nc.tensor.matmul(psSum[:], ones128[:, 0:1], sq[:, 0:128], start=True, stop=True))
            sp.s(lambda: nc.scalar.activation(out=S1()[:, 0:128], in_=psSum[:], func=AF.Sqrt))
            sp.s(lambda kh=kh: nc.scalar.activation(out=AUX()[:, 300:301], in_=AUX()[:, 256 + kh:257 + kh], func=AF.Sqrt))
            sp.v(lambda: nc.vector.tensor_scalar(S1()[:, 0:128], S1()[:, 0:128], AUX()[0:1, 300:301], None, ALU.mult))
            sp.v(lambda: nc.vector.tensor_scalar_mul(S1()[:, 0:128], S1()[:, 0:128], -1.0 / np.sqrt(128.0)))
            sp.t(lambda: nc.tensor.matmul(psB[:], ones1[:, 0:128], S1()[:, 0:128], start=True, stop=True))
            sp.v(lambda: nc.vector.tensor_copy(tmq[:], psB[:]))
            for kt in range(8):
                sp.t(lambda kh=kh, kt=kt, h=h: nc.tensor.matmul(
                    psS[:], bufA[:, kh * S + kt * 128:kh * S + kt * 128 + 128],
                    qsb[:, h * 128:(h + 1) * 128], start=True, stop=True))
                sp.v(lambda: nc.vector.scalar_tensor_tensor(expb[:], psS[:], 1.0 / np.sqrt(128.0), tmq[:], ALU.mult, ALU.add))
                sp.s(lambda: nc.scalar.activation(out=expb[:], in_=expb[:], func=AF.Exp))
                sp.v(lambda kt=kt: nc.vector.tensor_mul(expb[:], expb[:], cmsb[:, kt * 128:(kt + 1) * 128]))
                sp.t(lambda kt=kt: nc.tensor.matmul(psSum[:], ones128[:, 0:1], expb[:], start=(kt == 0), stop=(kt == 7)))
                sp.t(lambda kh=kh, kt=kt: nc.tensor.matmul(
                    psAV[:], bufH[:, (kh * 8 + kt) * 128:(kh * 8 + kt + 1) * 128], expb[:], start=(kt == 0), stop=(kt == 7)))
            sp.v(lambda: nc.vector.tensor_copy(S1()[:, 0:128], psSum[:]))
            sp.v(lambda: nc.vector.reciprocal(S1()[:, 0:128], S1()[:, 0:128]))
            sp.t(lambda: nc.tensor.matmul(psB[:], ones1[:, 0:128], S1()[:, 0:128], start=True, stop=True))
            sp.v(lambda: nc.vector.tensor_copy(tmq[:], psB[:]))
            sp.v(lambda h=h: nc.vector.tensor_mul(avfm[:, h * 128:(h + 1) * 128], psAV[:], tmq[:]))
        for m in range(DT_):
            band_ld(sp, nc, wtile, Wo, m, DT_, parts=2)
            for k in range(DT_):
                sp.t(lambda k=k, st=(k == 0), sp_=(k == DT_ - 1):
                     nc.tensor.matmul(psS[:], wtile[:, k * 128:(k + 1) * 128],
                                      avfm[:, k * 128:(k + 1) * 128], start=st, stop=sp_))
            sp.v(lambda m=m: nc.vector.tensor_add(x1sb[:, m * 128:(m + 1) * 128], psS[:], selmy[:, m * 128:(m + 1) * 128]))
        ln_stats2(nc, sp, x1sb, 128, DT_, psR, S1()[:, 0:128], S2()[:, 0:128], sq, ones128)
        sp.v(lambda: nc.vector.tensor_scalar_mul(S2()[:, 0:128], S2()[:, 0:128], 1.0 / D))
        sp.s(lambda: nc.scalar.activation(out=AUX()[:, 0:128], in_=S2()[:, 0:128], func=AF.Sqrt, bias=csb[0:1, 0:1]))
        sp.v(lambda: nc.vector.reciprocal(AUX()[:, 0:128], AUX()[:, 0:128]))
        sp.t(lambda: nc.tensor.matmul(psB[:], ones1[:, 0:128], AUX()[:, 0:128], start=True, stop=True))
        for k in range(DT_):
            sp.v(lambda k=k: nc.vector.scalar_tensor_tensor(
                h2sb[:, k * 128:(k + 1) * 128], x1sb[:, k * 128:(k + 1) * 128], smv[:, DT_ + k:DT_ + k + 1],
                psB[:], ALU.mult, ALU.mult))
        MIDOFF = 64 * 128
        for m in range(64):
            band_ld(sp, nc, wtile, Wg, m, DT_, parts=2)
            for k in range(DT_):
                sp.t(lambda k=k, st=(k == 0), sp_=(k == DT_ - 1):
                     nc.tensor.matmul(psS[:], wtile[:, k * 128:(k + 1) * 128],
                                      h2sb[:, k * 128:(k + 1) * 128], start=st, stop=sp_))
            sp.s(lambda: nc.scalar.activation(out=gsc[:], in_=psS[:], func=AF.Silu))
            band_ld(sp, nc, wtile, Wu, m, DT_, parts=2)
            for k in range(DT_):
                sp.t(lambda k=k, st=(k == 0), sp_=(k == DT_ - 1):
                     nc.tensor.matmul(psS[:], wtile[:, k * 128:(k + 1) * 128],
                                      h2sb[:, k * 128:(k + 1) * 128], start=st, stop=sp_))
            sp.v(lambda m=m: nc.vector.tensor_mul(bufH[:, MIDOFF + m * 128:MIDOFF + (m + 1) * 128], gsc[:], psS[:]))
        sp.s(lambda: nc.scalar.activation(out=S1()[:, 0:128], in_=scq_sb[:], func=AF.Sigmoid))
        sp.t(lambda: nc.tensor.matmul(psB[:], ones1[:, 0:128], S1()[:, 0:128], start=True, stop=True))
        for mo in range(DT_):
            for hf in range(2):
                for p0 in range(0, 32, 16):
                    sp.ld(wtile[:, p0 * 128:(p0 + 16) * 128],
                          Wd[(hf * 32 + p0) * 128:(hf * 32 + p0 + 16) * 128, mo * 128:(mo + 1) * 128]
                          .rearrange("(k p) j -> p k j", p=128))
                for k in range(32):
                    kk = hf * 32 + k
                    sp.t(lambda k=k, kk=kk, st=(kk == 0), sp_=(kk == 63):
                         nc.tensor.matmul(psS[:], wtile[:, k * 128:(k + 1) * 128],
                                          bufH[:, MIDOFF + kk * 128:MIDOFF + (kk + 1) * 128], start=st, stop=sp_))
            sp.v(lambda mo=mo: nc.vector.tensor_add(gsc[:], psS[:], x1sb[:, mo * 128:(mo + 1) * 128]))
            sp.v(lambda mo=mo: nc.vector.tensor_sub(gsc[:], gsc[:], selmy[:, mo * 128:(mo + 1) * 128]))
            sp.v(lambda: nc.vector.tensor_mul(gsc[:], gsc[:], psB[:]))
            sp.v(lambda mo=mo: nc.vector.tensor_add(gsc[:], gsc[:], selmy[:, mo * 128:(mo + 1) * 128]))
            sp.s(lambda: nc.scalar.activation(out=gsc[:], in_=gsc[:], func=AF.Silu))
            sp.st(Og[mo * 128:(mo + 1) * 128, :], gsc[:])
        sp.build(sem)
    return nc


def sel_from_D(resA, resD, inputs):
    mask = resD[0]["mask"]
    scs = [np.concatenate([resA[4 * b + i]["sc"][0] for i in range(4)]) for b in range(B)]
    xnb = [np.concatenate([resA[4 * b + i]["xn0"] for i in range(4)], axis=1) for b in range(B)]
    idxs, sels, poss, scse = [], [], [], []
    for b in range(B):
        m = mask[:, b * 32:(b + 1) * 32].reshape(4096)
        idx = np.where(m > 0.5)[0]
        assert idx.size == 512, idx.size
        idxs.append(idx)
        sels.append(xnb[b][:, idx])
        poss.append(idx.astype(np.float32))
        scse.append(scs[b][idx])
    sel = np.ascontiguousarray(np.concatenate(sels, axis=1))
    pos = np.concatenate(poss)
    scq = np.concatenate(scse)
    return idxs, sel, pos, scq


def prep_E_inputs(resA, resD, inputs):
    idxs, sel, pos, scq = sel_from_D(resA, resD, inputs)
    wq = np.ascontiguousarray(np.asarray(inputs["L0_ffn_wq"], np.float32))
    wk = np.ascontiguousarray(np.asarray(inputs["L0_ffn_wk"], np.float32))
    wv = np.ascontiguousarray(np.asarray(inputs["L0_ffn_wv"], np.float32))
    wo = np.ascontiguousarray(np.asarray(inputs["L0_ffn_wo"], np.float32))
    wg = np.ascontiguousarray(np.asarray(inputs["L0_ffn_w_gate"], np.float32))
    wu = np.ascontiguousarray(np.asarray(inputs["L0_ffn_w_up"], np.float32))
    wd = np.ascontiguousarray(np.asarray(inputs["L0_ffn_w_down"], np.float32))
    ln1 = colv(inputs["L0_ffn_ln1_w"]); ln2 = colv(inputs["L0_ffn_ln2_w"])
    bq = colv(inputs["L0_ffn_bq"]); bk = colv(inputs["L0_ffn_bk"], 8); bv = colv(inputs["L0_ffn_bv"], 8)
    posk = np.ascontiguousarray(pos.reshape(8, 128).T)
    maps = []
    for c in range(N_CORES):
        q0 = 128 * c
        kl = np.arange(128)[:, None]
        kt_ql = np.arange(8 * 128)[None, :]
        kglob = (kt_ql // 128) * 128 + kl
        qglob = q0 + (kt_ql % 128)
        cm = (kglob <= qglob).astype(np.float32)
        maps.append({
            "sel": sel, "selq": np.ascontiguousarray(sel[:, q0:q0 + 128]),
            "posq": np.ascontiguousarray(pos[q0:q0 + 128].reshape(128, 1)),
            "posk": posk, "scq": np.ascontiguousarray(scq[q0:q0 + 128].reshape(1, 128)),
            "cmask": np.ascontiguousarray(cm),
            "wq": wq, "wk": wk, "wv": wv, "wo": wo, "wg": wg, "wu": wu, "wd": wd,
            "ln1": ln1, "ln2": ln2, "bq": bq, "bk": bk, "bv": bv,
        })
    return maps, idxs, sel, pos, scq


def build_F():
    nc = bass.Bass("TRN2", target_bir_lowering=False, debug=False)
    Ix = nc.dram_tensor("x", [D, NTOK], F32, kind="ExternalInput").ap()
    Is0 = nc.dram_tensor("s0", [D, NTOK], F32, kind="ExternalInput").ap()
    Ig = nc.dram_tensor("g", [D, NTOK], F32, kind="ExternalInput").ap()
    Ox1 = nc.dram_tensor("x1", [D, NTOK], F32, kind="ExternalOutput").ap()
    Oxn = nc.dram_tensor("xn1", [D, NTOK], F32, kind="ExternalOutput").ap()
    sp = SP(nc)
    with ExitStack() as es:
        E = es.enter_context
        x1sb = E(nc.sbuf_tensor("x1F", [128, DT_ * NTOK], F32))
        tsb = E(nc.sbuf_tensor("tF", [128, NTOK], F32))
        sq = E(nc.sbuf_tensor("sqF", [128, NCH], F32))
        t1 = E(nc.sbuf_tensor("t1F", [128, NCH], F32))
        vs = E(nc.sbuf_tensor("vsF", [1, NTOK * 5], F32))
        ones128 = E(nc.sbuf_tensor("o128F", [128, 1], F32))
        ones1 = E(nc.sbuf_tensor("o1F", [1, 128], F32))
        csb = E(nc.sbuf_tensor("csbF", [128, 1], F32))
        psR = E(nc.psum_tensor("psRF", [1, NCH], F32))
        psB1 = E(nc.psum_tensor("psB1F", [128, NCH], F32))
        psB2 = E(nc.psum_tensor("psB2F", [128, NCH], F32))
        sem = E(nc.semaphore("gs"))
        S1 = lambda: vs[:, 0:NTOK]
        S2 = lambda: vs[:, NTOK:2 * NTOK]
        MU = lambda: vs[:, 2 * NTOK:3 * NTOK]
        RSTD = lambda: vs[:, 3 * NTOK:4 * NTOK]
        NMR = lambda: vs[:, 4 * NTOK:5 * NTOK]
        sp.v(lambda: nc.vector.memset(ones128[:], 1.0))
        sp.v(lambda: nc.vector.memset(ones1[:], 1.0))
        sp.v(lambda: nc.vector.memset(csb[:, 0:1], LN_EPS))
        for k in range(DT_):
            sp.ld(x1sb[:, k * NTOK:(k + 1) * NTOK], Is0[k * 128:(k + 1) * 128, :])
            sp.ld(tsb[:], Ig[k * 128:(k + 1) * 128, :])
            sp.v(lambda k=k: nc.vector.tensor_mul(x1sb[:, k * NTOK:(k + 1) * NTOK],
                                                  x1sb[:, k * NTOK:(k + 1) * NTOK], tsb[:]))
            sp.ld(tsb[:], Ix[k * 128:(k + 1) * 128, :])
            sp.v(lambda k=k: nc.vector.tensor_add(x1sb[:, k * NTOK:(k + 1) * NTOK],
                                                  x1sb[:, k * NTOK:(k + 1) * NTOK], tsb[:]))
            sp.st(Ox1[k * 128:(k + 1) * 128, :], x1sb[:, k * NTOK:(k + 1) * NTOK])
        ln_stats(nc, sp, x1sb, NTOK, DT_, psR, S1(), S2(), sq, ones128)
        dinv = 1.0 / D
        sp.v(lambda: nc.vector.tensor_scalar_mul(MU(), S1(), dinv))
        sp.v(lambda: nc.vector.tensor_mul(S1(), MU(), MU()))
        sp.v(lambda: nc.vector.tensor_scalar_mul(S2(), S2(), dinv))
        sp.v(lambda: nc.vector.tensor_sub(S2(), S2(), S1()))
        sp.s(lambda: nc.scalar.activation(out=RSTD(), in_=S2(), func=AF.Sqrt, bias=csb[0:1, 0:1]))
        sp.v(lambda: nc.vector.reciprocal(RSTD(), RSTD()))
        sp.v(lambda: nc.vector.tensor_mul(NMR(), MU(), RSTD()))
        sp.v(lambda: nc.vector.tensor_scalar_mul(NMR(), NMR(), -1.0))
        for ch in range(NTOK // NCH):
            bcast_row(nc, sp, psB1, ones1, RSTD(), ch * NCH, NCH)
            bcast_row(nc, sp, psB2, ones1, NMR(), ch * NCH, NCH)
            for k in range(DT_):
                c0 = k * NTOK + ch * NCH
                sp.v(lambda c0=c0: nc.vector.tensor_mul(t1[:, 0:NCH], x1sb[:, c0:c0 + NCH], psB1[:, 0:NCH]))
                sp.v(lambda c0=c0: nc.vector.tensor_add(t1[:, 0:NCH], t1[:, 0:NCH], psB2[:, 0:NCH]))
                sp.st(Oxn[k * 128:(k + 1) * 128, ch * NCH:(ch + 1) * NCH], t1[:, 0:NCH])
        sp.build(sem)
    return nc


KPAD = 1536

def build_G1():
    nc = bass.Bass("TRN2", target_bir_lowering=False, debug=False)
    Ixn = nc.dram_tensor("xn1", [D, NTOK], F32, kind="ExternalInput").ap()
    Ixa = nc.dram_tensor("x1a", [D, KPAD], F32, kind="ExternalInput").ap()
    Iqp = nc.dram_tensor("qpos", [128, 8], F32, kind="ExternalInput").ap()
    Ikp = nc.dram_tensor("kpos", [128, 12], F32, kind="ExternalInput").ap()
    Ikm = nc.dram_tensor("kmask", [128, 12], F32, kind="ExternalInput").ap()
    Ikmr = nc.dram_tensor("kmaskr", [1, KPAD], F32, kind="ExternalInput").ap()
    Wq = nc.dram_tensor("wqt", [D, D], F32, kind="ExternalInput").ap()
    Wk = nc.dram_tensor("wkt", [D, D], F32, kind="ExternalInput").ap()
    Wv = nc.dram_tensor("wvt", [D, D], F32, kind="ExternalInput").ap()
    Wo = nc.dram_tensor("wot", [D, D], F32, kind="ExternalInput").ap()
    Vbq = nc.dram_tensor("bq", [128, DT_], F32, kind="ExternalInput").ap()
    Vbk = nc.dram_tensor("bk", [128, DT_], F32, kind="ExternalInput").ap()
    Vbv = nc.dram_tensor("bv", [128, DT_], F32, kind="ExternalInput").ap()
    Vbo = nc.dram_tensor("bo", [128, DT_], F32, kind="ExternalInput").ap()
    Os1 = nc.dram_tensor("s1", [D, NTOK], F32, kind="ExternalOutput").ap()
    Kscr = nc.dram_tensor("kscr", [D, KPAD], F32, kind="Internal").ap()
    Vscr = nc.dram_tensor("vscr", [KPAD, D], F32, kind="Internal").ap()

    sp = SP(nc)
    NH = 16
    with ExitStack() as es:
        E = es.enter_context
        xasb = E(nc.sbuf_tensor("xaG", [128, DT_ * 128], F32))
        kvsb = E(nc.sbuf_tensor("kvG", [128, DT_ * 128], F32))
        qsb = E(nc.sbuf_tensor("qG", [128, DT_ * NCH], F32))
        xnsb = E(nc.sbuf_tensor("xnG", [128, DT_ * NCH], F32))
        avsb = E(nc.sbuf_tensor("avG", [128, DT_ * NCH], F32))
        wtile = E(nc.sbuf_tensor("wtG", [128, DT_ * 128], F32))
        ktile = E(nc.sbuf_tensor("ktG", [128, 128], F32))
        vtt = E(nc.sbuf_tensor("vttG", [128, 128], F32))
        trig = E(nc.sbuf_tensor("trigG", [128, 128], F32))
        tmq = E(nc.sbuf_tensor("tmqG", [128, 128], F32))
        tmq2 = E(nc.sbuf_tensor("tmq2G", [128, 128], F32))
        gsc = E(nc.sbuf_tensor("gscG", [128, 128], F32))
        expb = E(nc.sbuf_tensor("expbG", [128, NCH], F32))
        cneg = E(nc.sbuf_tensor("cnegG", [128, NCH], F32))
        sq = E(nc.sbuf_tensor("sqG", [128, NCH], F32))
        idf = E(nc.sbuf_tensor("idfG", [128, 128], F32))
        pii = E(nc.sbuf_tensor("piiG", [128, 1], I32))
        fii = E(nc.sbuf_tensor("fiiG", [128, 128], I32))
        nint = E(nc.sbuf_tensor("nintG", [128, 64], I32))
        angs = E(nc.sbuf_tensor("angsG", [128, 3 * 64], F32))
        vecs = E(nc.sbuf_tensor("vecsG", [1, 3 * NTOK + 64], F32))
        smv = E(nc.sbuf_tensor("smvG", [128, 4 * DT_ + 8 + 12 + 12], F32))
        csb = E(nc.sbuf_tensor("csbG", [128, 2], F32))
        ones128 = E(nc.sbuf_tensor("o128G", [128, 1], F32))
        ones1 = E(nc.sbuf_tensor("o1G", [1, 128], F32))
        ifr = E(nc.sbuf_tensor("ifrG", [1, 64], F32))
        ifri = E(nc.sbuf_tensor("ifriG", [1, 64], I32))
        kmr = E(nc.sbuf_tensor("kmrG", [1, KPAD], F32))
        psA = E(nc.psum_tensor("psAG", [128, NCH], F32))
        psS = E(nc.psum_tensor("psSG", [128, NCH], F32))
        psAV = E(nc.psum_tensor("psAVG", [128, NCH], F32))
        psSum = E(nc.psum_tensor("psSumG", [1, NCH], F32))
        psB = E(nc.psum_tensor("psBG", [128, NCH], F32))
        psT = E(nc.psum_tensor("psTG", [128, 128], F32))
        sem = E(nc.semaphore("gs"))

        CS = lambda: vecs[:, 0:NTOK]
        QN = lambda: vecs[:, NTOK:2 * NTOK]
        AUX = lambda: vecs[:, 2 * NTOK:3 * NTOK]
        KNM = lambda: vecs[:, 3 * NTOK:3 * NTOK + 64]

        OBQ, OBK, OBV, OBO = 0, DT_, 2 * DT_, 3 * DT_
        OQP, OKP, OKM = 4 * DT_, 4 * DT_ + 8, 4 * DT_ + 8 + 12

        sp.ld(smv[:, OBQ:OBQ + DT_], Vbq[:])
        sp.ld(smv[:, OBK:OBK + DT_], Vbk[:])
        sp.ld(smv[:, OBV:OBV + DT_], Vbv[:])
        sp.ld(smv[:, OBO:OBO + DT_], Vbo[:])
        sp.ld(smv[:, OQP:OQP + 8], Iqp[:])
        sp.ld(smv[:, OKP:OKP + 12], Ikp[:])
        sp.ld(smv[:, OKM:OKM + 12], Ikm[:])
        sp.ld(kmr[:], Ikmr[:])
        sp.v(lambda: nc.vector.memset(ones128[:], 1.0))
        sp.v(lambda: nc.vector.memset(ones1[:], 1.0))
        sp.v(lambda: nc.vector.memset(csb[:, 1:2], -np.log(10000.0) / 64.0))
        sp.g(lambda: nc.gpsimd.iota(pii[:], pattern=[[0, 1]], base=0, channel_multiplier=1))
        sp.g(lambda: nc.gpsimd.iota(fii[:], pattern=[[1, 128]], base=0, channel_multiplier=0))
        sp.v(lambda: nc.vector.tensor_copy(idf[:], fii[:]))
        sp.v(lambda: nc.vector.tensor_copy(angs[:, 0:1], pii[:]))
        sp.v(lambda: nc.vector.tensor_scalar(idf[:], idf[:], angs[:, 0:1], None, ALU.is_equal))
        sp.g(lambda: nc.gpsimd.iota(ifri[:], pattern=[[1, 64]], base=0, channel_multiplier=0))
        sp.v(lambda: nc.vector.tensor_copy(ifr[:], ifri[:]))
        sp.s(lambda: nc.scalar.activation(out=ifr[:], in_=ifr[:], func=AF.Exp, scale=csb[0:1, 1:2]))
        sp.v(lambda: nc.vector.memset(KNM()[:, 0:64], 0.0))

        def rope_trig1(pos_col):
            sp.t(lambda: nc.tensor.matmul(psB[:, 0:64], ones1[:, 0:128], ifr[:], start=True, stop=True))
            A_ = angs[:, 0:64]; R_ = angs[:, 64:128]; W_ = angs[:, 128:192]
            sp.v(lambda pos_col=pos_col: nc.vector.tensor_scalar(A_, psB[:, 0:64], pos_col, None, ALU.mult))
            for (dst, phase) in ((64, 0.0), (0, np.pi / 2)):
                sp.v(lambda phase=phase: nc.vector.tensor_scalar(W_, A_, 1.0 / ROPE_C1, phase / ROPE_C1, ALU.mult, ALU.add))
                sp.v(lambda: nc.vector.tensor_copy(nint[:], W_))
                sp.v(lambda: nc.vector.tensor_copy(W_, nint[:]))
                sp.v(lambda phase=phase: nc.vector.tensor_scalar(R_, A_, 1.0, phase, ALU.mult, ALU.add))
                sp.v(lambda: nc.vector.scalar_tensor_tensor(R_, W_, -ROPE_C1, R_, ALU.mult, ALU.add))
                sp.v(lambda: nc.vector.scalar_tensor_tensor(R_, W_, -ROPE_C2, R_, ALU.mult, ALU.add))
                sp.s(lambda dst=dst: nc.scalar.activation(out=trig[:, dst:dst + 64], in_=R_, func=AF.Sin))

        def rope_tile1(fm_ap):
            sp.t(lambda: nc.tensor.transpose(psT[:], fm_ap, idf[:]))
            sp.v(lambda: nc.vector.tensor_copy(tmq[:], psT[:]))
            C_ = trig[:, 0:64]; Sn = trig[:, 64:128]
            sp.v(lambda: nc.vector.tensor_mul(tmq2[:, 0:64], tmq[:, 0:64], C_))
            sp.v(lambda: nc.vector.tensor_mul(gsc[:, 0:64], tmq[:, 64:128], Sn))
            sp.v(lambda: nc.vector.tensor_sub(tmq2[:, 0:64], tmq2[:, 0:64], gsc[:, 0:64]))
            sp.v(lambda: nc.vector.tensor_mul(tmq2[:, 64:128], tmq[:, 64:128], C_))
            sp.v(lambda: nc.vector.tensor_mul(gsc[:, 0:64], tmq[:, 0:64], Sn))
            sp.v(lambda: nc.vector.tensor_add(tmq2[:, 64:128], tmq2[:, 64:128], gsc[:, 0:64]))
            sp.t(lambda: nc.tensor.transpose(psT[:], tmq2[:], idf[:]))
            sp.v(lambda fm_ap=fm_ap: nc.vector.tensor_copy(fm_ap, psT[:]))

        for kt in range(12):
            for p0 in range(0, DT_, 8):
                sp.ld(xasb[:, p0 * 128:(p0 + 8) * 128],
                      Ixa[p0 * 128:(p0 + 8) * 128, kt * 128:(kt + 1) * 128].rearrange("(k p) j -> p k j", p=128))
            rope_trig1(smv[:, OKP + kt:OKP + kt + 1])
            for (Wt, bof, isk) in ((Wk, OBK, True), (Wv, OBV, False)):
                for m in range(DT_):
                    band_ld(sp, nc, wtile, Wt, m, DT_, parts=2)
                    for k in range(DT_):
                        sp.t(lambda k=k, st=(k == 0), sp_=(k == DT_ - 1):
                             nc.tensor.matmul(psS[:, 0:128], wtile[:, k * 128:(k + 1) * 128],
                                              xasb[:, k * 128:(k + 1) * 128], start=st, stop=sp_))
                    sp.v(lambda m=m, bof=bof: nc.vector.tensor_scalar_add(
                        kvsb[:, m * 128:(m + 1) * 128], psS[:, 0:128], smv[:, bof + m:bof + m + 1]))
                if isk:
                    for h in range(NH):
                        rope_tile1(kvsb[:, h * 128:(h + 1) * 128])
                        sp.s(lambda h=h: nc.scalar.activation(out=sq[:, 0:128], in_=kvsb[:, h * 128:(h + 1) * 128], func=AF.Square))
                        sp.t(lambda: nc.tensor.matmul(psSum[:, 0:128], ones128[:, 0:1], sq[:, 0:128], start=True, stop=True))
                        sp.v(lambda kt=kt: nc.vector.tensor_mul(sq[0:1, 0:128], psSum[0:1, 0:128], kmr[0:1, kt * 128:(kt + 1) * 128]))
                        sp.v(lambda h=h: nc.vector.reduce_max(AUX()[:, h:h + 1], sq[0:1, 0:128], axis=AX.X))
                        sp.v(lambda h=h: nc.vector.tensor_max(KNM()[:, h:h + 1], KNM()[:, h:h + 1], AUX()[:, h:h + 1]))
                    for k in range(DT_):
                        sp.st(Kscr[k * 128:(k + 1) * 128, kt * 128:(kt + 1) * 128], kvsb[:, k * 128:(k + 1) * 128])
                else:
                    for h in range(NH):
                        sp.t(lambda h=h: nc.tensor.transpose(psT[:], kvsb[:, h * 128:(h + 1) * 128], idf[:]))
                        sp.v(lambda: nc.vector.tensor_copy(vtt[:], psT[:]))
                        sp.st(Vscr[kt * 128:(kt + 1) * 128, h * 128:(h + 1) * 128], vtt[:])
        sp.s(lambda: nc.scalar.activation(out=KNM()[:, 0:16], in_=KNM()[:, 0:16], func=AF.Sqrt))

        for ch in range(NTOK // NCH):
            t0 = ch * NCH
            for k in range(DT_):
                sp.ld(xnsb[:, k * NCH:(k + 1) * NCH], Ixn[k * 128:(k + 1) * 128, t0:t0 + NCH])
            for m in range(DT_):
                band_ld(sp, nc, wtile, Wq, m, DT_, parts=2)
                for k in range(DT_):
                    sp.t(lambda k=k, st=(k == 0), sp_=(k == DT_ - 1):
                         nc.tensor.matmul(psA[:, 0:NCH], wtile[:, k * 128:(k + 1) * 128],
                                          xnsb[:, k * NCH:(k + 1) * NCH], start=st, stop=sp_))
                sp.v(lambda m=m: nc.vector.tensor_scalar_add(qsb[:, m * NCH:(m + 1) * NCH], psA[:, 0:NCH], smv[:, OBQ + m:OBQ + m + 1]))
            for sti in range(NCH // 128):
                rope_trig1(smv[:, OQP + ch * (NCH // 128) + sti:OQP + ch * (NCH // 128) + sti + 1])
                for h in range(NH):
                    rope_tile1(qsb[:, h * NCH + sti * 128:h * NCH + sti * 128 + 128])
            for h in range(NH):
                sp.s(lambda h=h: nc.scalar.activation(out=sq[:, 0:NCH], in_=qsb[:, h * NCH:(h + 1) * NCH], func=AF.Square))
                sp.t(lambda: nc.tensor.matmul(psSum[:, 0:NCH], ones128[:, 0:1], sq[:, 0:NCH], start=True, stop=True))
                sp.s(lambda: nc.scalar.activation(out=QN()[:, 0:NCH], in_=psSum[:, 0:NCH], func=AF.Sqrt))
                sp.v(lambda h=h: nc.vector.tensor_scalar(QN()[:, 0:NCH], QN()[:, 0:NCH], KNM()[0:1, h:h + 1], None, ALU.mult))
                sp.v(lambda: nc.vector.tensor_scalar_mul(QN()[:, 0:NCH], QN()[:, 0:NCH], -1.0 / np.sqrt(128.0)))
                sp.t(lambda: nc.tensor.matmul(psB[:, 0:NCH], ones1[:, 0:128], QN()[:, 0:NCH], start=True, stop=True))
                sp.v(lambda: nc.vector.tensor_copy(cneg[:, 0:NCH], psB[:, 0:NCH]))
                for kt in range(12):
                    sp.ld(ktile[:], Kscr[h * 128:(h + 1) * 128, kt * 128:(kt + 1) * 128])
                    sp.ld(vtt[:], Vscr[kt * 128:(kt + 1) * 128, h * 128:(h + 1) * 128])
                    sp.t(lambda: nc.tensor.matmul(psS[:, 0:NCH], ktile[:], qsb[:, h * NCH:(h + 1) * NCH] if False else qsb[:, 0:NCH], start=True, stop=True))
                    sp.ops.pop()
                    sp.t(lambda h=h: nc.tensor.matmul(psS[:, 0:NCH], ktile[:], qsb[:, h * NCH:(h + 1) * NCH], start=True, stop=True))
                    sp.v(lambda: nc.vector.scalar_tensor_tensor(expb[:, 0:NCH], psS[:, 0:NCH], 1.0 / np.sqrt(128.0), cneg[:, 0:NCH], ALU.mult, ALU.add))
                    sp.s(lambda: nc.scalar.activation(out=expb[:, 0:NCH], in_=expb[:, 0:NCH], func=AF.Exp))
                    sp.v(lambda kt=kt: nc.vector.tensor_scalar(expb[:, 0:NCH], expb[:, 0:NCH], smv[:, OKM + kt:OKM + kt + 1], None, ALU.mult))
                    sp.t(lambda kt=kt: nc.tensor.matmul(psSum[:, 0:NCH], ones128[:, 0:1], expb[:, 0:NCH], start=(kt == 0), stop=(kt == 11)))
                    sp.t(lambda kt=kt: nc.tensor.matmul(psAV[:, 0:NCH], vtt[:], expb[:, 0:NCH], start=(kt == 0), stop=(kt == 11)))
                sp.v(lambda: nc.vector.tensor_copy(CS()[:, 0:NCH], psSum[:, 0:NCH]))
                sp.v(lambda: nc.vector.reciprocal(CS()[:, 0:NCH], CS()[:, 0:NCH]))
                sp.t(lambda: nc.tensor.matmul(psB[:, 0:NCH], ones1[:, 0:128], CS()[:, 0:NCH], start=True, stop=True))
                sp.v(lambda: nc.vector.tensor_copy(cneg[:, 0:NCH], psB[:, 0:NCH]))
                sp.v(lambda h=h: nc.vector.tensor_mul(avsb[:, h * NCH:(h + 1) * NCH], psAV[:, 0:NCH], cneg[:, 0:NCH]))
            for m in range(DT_):
                band_ld(sp, nc, wtile, Wo, m, DT_, parts=2)
                for k in range(DT_):
                    sp.t(lambda k=k, st=(k == 0), sp_=(k == DT_ - 1):
                         nc.tensor.matmul(psA[:, 0:NCH], wtile[:, k * 128:(k + 1) * 128],
                                          avsb[:, k * NCH:(k + 1) * NCH], start=st, stop=sp_))
                sp.v(lambda m=m: nc.vector.tensor_scalar_add(expb[:, 0:NCH], psA[:, 0:NCH], smv[:, OBO + m:OBO + m + 1]))
                sp.st(Os1[m * 128:(m + 1) * 128, t0:t0 + NCH], expb[:, 0:NCH])
        sp.build(sem)
    return nc



def build_G2():
    nc = bass.Bass("TRN2", target_bir_lowering=False, debug=False)
    Ixn = nc.dram_tensor("xn1", [D, NTOK], F32, kind="ExternalInput").ap()
    Ix1 = nc.dram_tensor("x1", [D, NTOK], F32, kind="ExternalInput").ap()
    Is1 = nc.dram_tensor("s1", [D, NTOK], F32, kind="ExternalInput").ap()
    Wg = nc.dram_tensor("wqg", [D, 5632], I32, kind="ExternalInput").ap()
    Wu = nc.dram_tensor("wqu", [D, 5632], I32, kind="ExternalInput").ap()
    Wd = nc.dram_tensor("wqd", [5632, D], I32, kind="ExternalInput").ap()
    Sg = nc.dram_tensor("sg", [1, 5632], F32, kind="ExternalInput").ap()
    Su = nc.dram_tensor("su", [1, 5632], F32, kind="ExternalInput").ap()
    Sd = nc.dram_tensor("sd", [1, D], F32, kind="ExternalInput").ap()
    Oout = nc.dram_tensor("out", [D, NTOK], F32, kind="ExternalOutput").ap()

    sp = SP(nc)
    TCG = 512
    with ExitStack() as es:
        E = es.enter_context
        xnsb = E(nc.sbuf_tensor("xnG2", [128, DT_ * TCG], F32))
        midb = E(nc.sbuf_tensor("midG2", [128, NMID * TCG], F32))
        wti = E(nc.sbuf_tensor("wtiG2", [128, DT_ * 128], I32))
        wtf = E(nc.sbuf_tensor("wtfG2", [128, DT_ * 128], F32))
        gsc = E(nc.sbuf_tensor("gscG2", [128, TCG], F32))
        osc = E(nc.sbuf_tensor("oscG2", [128, TCG], F32))
        ssb = E(nc.sbuf_tensor("ssbG2", [1, 5632 * 2 + D], F32))
        ones1 = E(nc.sbuf_tensor("o1G2", [1, 128], F32))
        psA = E(nc.psum_tensor("psAG2", [128, NCH], F32))
        psB = E(nc.psum_tensor("psBG2", [128, 128], F32))
        sem = E(nc.semaphore("gs"))

        sp.ld(ssb[:, 0:5632], Sg[:])
        sp.ld(ssb[:, 5632:2 * 5632], Su[:])
        sp.ld(ssb[:, 2 * 5632:2 * 5632 + D], Sd[:])
        sp.v(lambda: nc.vector.memset(ones1[:], 1.0))

        def deq(W_, m, k0, nk, soff, scol):
            hstep = max(1, nk // 2)
            for p0 in range(0, nk, hstep):
                n_ = min(hstep, nk - p0)
                sp.ld(wti[:, p0 * 128:(p0 + n_) * 128],
                      W_[(k0 + p0) * 128:(k0 + p0 + n_) * 128, m * 128:(m + 1) * 128]
                      .rearrange("(k p) j -> p k j", p=128))
            sp.t(lambda soff=soff, scol=scol: nc.tensor.matmul(
                psB[:], ones1[:, 0:128], ssb[:, soff + scol * 128:soff + (scol + 1) * 128], start=True, stop=True))
            sp.v(lambda nk=nk: nc.vector.tensor_copy(wtf[:, 0:nk * 128], wti[:, 0:nk * 128]))
            sp.v(lambda nk=nk: nc.vector.tensor_scalar_add(wtf[:, 0:nk * 128], wtf[:, 0:nk * 128], -128.0))
            sp.v(lambda nk=nk: nc.vector.tensor_mul(
                wtf[:, 0:nk * 128].rearrange("p (k j) -> p k j", j=128),
                wtf[:, 0:nk * 128].rearrange("p (k j) -> p k j", j=128),
                psB[:].unsqueeze(1).broadcast_to([128, nk, 128])))

        for ch in range(NTOK // TCG):
            t0 = ch * TCG
            for k in range(DT_):
                sp.ld(xnsb[:, k * TCG:(k + 1) * TCG], Ixn[k * 128:(k + 1) * 128, t0:t0 + TCG])
            for m in range(NMID):
                deq(Wg, m, 0, DT_, 0, m)
                for k in range(DT_):
                    sp.t(lambda k=k, st=(k == 0), sp_=(k == DT_ - 1):
                         nc.tensor.matmul(psA[:, 0:TCG], wtf[:, k * 128:(k + 1) * 128],
                                          xnsb[:, k * TCG:(k + 1) * TCG], start=st, stop=sp_))
                sp.s(lambda: nc.scalar.activation(out=gsc[:], in_=psA[:, 0:TCG], func=AF.Silu))
                deq(Wu, m, 0, DT_, 5632, m)
                for k in range(DT_):
                    sp.t(lambda k=k, st=(k == 0), sp_=(k == DT_ - 1):
                         nc.tensor.matmul(psA[:, 0:TCG], wtf[:, k * 128:(k + 1) * 128],
                                          xnsb[:, k * TCG:(k + 1) * TCG], start=st, stop=sp_))
                sp.v(lambda m=m: nc.vector.tensor_mul(midb[:, m * TCG:(m + 1) * TCG], gsc[:], psA[:, 0:TCG]))
            for mo in range(DT_):
                for qtr in range(4):
                    k0 = qtr * 11
                    deq(Wd, mo, k0, 11, 2 * 5632, mo)
                    for k in range(11):
                        kk = k0 + k
                        sp.t(lambda k=k, kk=kk, st=(kk == 0), sp_=(kk == NMID - 1):
                             nc.tensor.matmul(psA[:, 0:TCG], wtf[:, k * 128:(k + 1) * 128],
                                              midb[:, kk * TCG:(kk + 1) * TCG], start=st, stop=sp_))
                sp.s(lambda: nc.scalar.activation(out=gsc[:], in_=psA[:, 0:TCG], func=AF.Silu))
                sp.ld(osc[:], Is1[mo * 128:(mo + 1) * 128, t0:t0 + TCG])
                sp.v(lambda: nc.vector.tensor_mul(gsc[:], gsc[:], osc[:]))
                sp.ld(osc[:], Ix1[mo * 128:(mo + 1) * 128, t0:t0 + TCG])
                sp.v(lambda: nc.vector.tensor_add(gsc[:], gsc[:], osc[:]))
                sp.st(Oout[mo * 128:(mo + 1) * 128, t0:t0 + TCG], gsc[:])
        sp.build(sem)
    return nc



ALLOWED = np.concatenate([np.arange(0, 410), np.arange(3072, 4096)])

def rope_perm():
    p = np.zeros(D, np.int64)
    for h in range(16):
        p[h * 128:h * 128 + 64] = h * 128 + np.arange(0, 128, 2)
        p[h * 128 + 64:(h + 1) * 128] = h * 128 + np.arange(1, 128, 2)
    return p


def prep_F_inputs(inputs, resC, resE, resA, idxs):
    x = np.asarray(inputs["L0_attn_hidden_states"], np.float32).reshape(B * T, D)
    gb = [np.concatenate([resA[4 * b + i]["gb"] for i in range(4)], axis=1).copy() for b in range(B)]
    gated = np.concatenate([resE[c]["gated"] for c in range(N_CORES)], axis=1)
    for b in range(B):
        gb[b][:, idxs[b]] = gated[:, b * 512:(b + 1) * 512]
    maps = []
    for c in range(N_CORES):
        b, a = c // 4, c % 4
        maps.append({"x": fm(x[c * NTOK:(c + 1) * NTOK]),
                     "s0": resC[c]["s0"],
                     "g": np.ascontiguousarray(gb[b][:, a * NTOK:(a + 1) * NTOK])})
    return maps


def prep_G1_inputs(inputs, resF):
    perm = rope_perm()
    wqt = fm(inputs["L1_attn_Wq"])[:, perm]
    wkt = fm(inputs["L1_attn_Wk"])[:, perm]
    wvt = fm(inputs["L1_attn_Wv"])
    wot = fm(inputs["L1_attn_Wo"])
    bq = colv(np.asarray(inputs["L1_attn_bq"], np.float32)[perm])
    bk = colv(np.asarray(inputs["L1_attn_bk"], np.float32)[perm])
    bv = colv(inputs["L1_attn_bv"])
    bo = colv(inputs["L1_attn_bo"])
    kpos = np.zeros(KPAD, np.float32); kpos[:1434] = ALLOWED
    kmask = np.zeros(KPAD, np.float32); kmask[:1434] = 1.0
    kposc = np.ascontiguousarray(kpos.reshape(12, 128).T)
    kmaskc = np.ascontiguousarray(kmask.reshape(12, 128).T)
    kmaskr = np.ascontiguousarray(kmask.reshape(1, KPAD))
    maps = []
    for c in range(N_CORES):
        b, a = c // 4, c % 4
        xn1b = np.concatenate([resF[4 * b + i]["xn1"] for i in range(4)], axis=1)
        x1a = np.zeros((D, KPAD), np.float32)
        x1a[:, :1434] = xn1b[:, ALLOWED]
        qpos = (a * NTOK + np.arange(NTOK, dtype=np.float32)).reshape(8, 128).T
        maps.append({"xn1": resF[c]["xn1"], "x1a": np.ascontiguousarray(x1a),
                     "qpos": np.ascontiguousarray(qpos), "kpos": kposc, "kmask": kmaskc,
                     "kmaskr": kmaskr,
                     "wqt": wqt, "wkt": wkt, "wvt": wvt, "wot": wot,
                     "bq": bq, "bk": bk, "bv": bv, "bo": bo})
    return maps


def prep_G2_inputs(inputs, resF, resG1):
    wqg = np.ascontiguousarray(np.asarray(inputs["L1_ffn_wq_gate"]).T.astype(np.int32))
    wqu = np.ascontiguousarray(np.asarray(inputs["L1_ffn_wq_up"]).T.astype(np.int32))
    wqd = np.ascontiguousarray(np.asarray(inputs["L1_ffn_wq_down"]).T.astype(np.int32))
    sg = np.asarray(inputs["L1_ffn_s_gate"], np.float32).reshape(1, 5632)
    su = np.asarray(inputs["L1_ffn_s_up"], np.float32).reshape(1, 5632)
    sd = np.asarray(inputs["L1_ffn_s_down"], np.float32).reshape(1, D)
    maps = []
    for c in range(N_CORES):
        maps.append({"xn1": resF[c]["xn1"], "x1": resF[c]["x1"], "s1": resG1[c]["s1"],
                     "wqg": wqg, "wqu": wqu, "wqd": wqd, "sg": sg, "su": su, "sd": sd})
    return maps


def full_forward(inputs):
    resA = run_A(inputs)
    resB = run_bass_kernel_spmd(build_B(), prep_B_inputs(resA, inputs), core_ids=list(range(N_CORES))).results
    resC = run_bass_kernel_spmd(build_C(), prep_C_inputs(resA, resB, inputs), core_ids=list(range(N_CORES))).results
    scl = np.ascontiguousarray(np.concatenate(
        [np.concatenate([resA[4 * b + i]["sc"][0] for i in range(4)]).reshape(128, 32) for b in range(B)], axis=1))
    resD = run_bass_kernel_spmd(build_D(), [{"sc": scl}] * N_CORES, core_ids=list(range(N_CORES))).results
    mapsE, idxs, sel, pos, scq = prep_E_inputs(resA, resD, inputs)
    resE = run_bass_kernel_spmd(build_E(), mapsE, core_ids=list(range(N_CORES))).results
    resF = run_bass_kernel_spmd(build_F(), prep_F_inputs(inputs, resC, resE, resA, idxs), core_ids=list(range(N_CORES))).results
    resG1 = run_bass_kernel_spmd(build_G1(), prep_G1_inputs(inputs, resF), core_ids=list(range(N_CORES))).results
    resG2 = run_bass_kernel_spmd(build_G2(), prep_G2_inputs(inputs, resF, resG1), core_ids=list(range(N_CORES))).results
    out = np.concatenate([resG2[c]["out"].T for c in range(N_CORES)], axis=0)
    return np.ascontiguousarray(out.reshape(B, T, D))



def kernel(**inputs):
    return full_forward(inputs)
